# version 01
# baseline (speedup 1.0000x reference)
import numpy as np
import ml_dtypes
from contextlib import ExitStack

import concourse.bass as bass
import concourse.mybir as mybir
from concourse.bass_utils import run_bass_kernel_spmd

F32 = mybir.dt.float32
BF16 = mybir.dt.bfloat16
I32 = mybir.dt.int32
AF = mybir.ActivationFunctionType
ALU = mybir.AluOpType
AX = mybir.AxisListType

D = 1024
S = 4096
NCORES = 8
CH = 512
NCH = S // CH
NDT = D // 128

bf = ml_dtypes.bfloat16


class K:
    SEM_SPAN = 24000

    def __init__(self):
        self.nc = bass.Bass("TRN2", target_bir_lowering=False)
        self.ctx = ExitStack()
        self.script = []
        self.inputs = {}

    def dram(self, name, shape, dt, kind):
        return self.nc.dram_tensor(name, shape, dt, kind=kind).ap()

    def din(self, name, arr):
        a = np.ascontiguousarray(arr)
        t = self.nc.dram_tensor(name, list(a.shape), np_to_dt(a.dtype), kind="ExternalInput").ap()
        self.inputs[name] = a
        return t

    def sb(self, name, p, f, dt):
        return self.ctx.enter_context(self.nc.sbuf_tensor(name, [p, f], dt))

    def ps(self, name, p, f, dt=F32):
        return self.ctx.enter_context(self.nc.psum_tensor(name, [p, f], dt))

    def op(self, eng, meth, *args, inc=1, **kwargs):
        self.script.append((eng, meth, args, kwargs, inc))

    def dma(self, out, in_, **kw):
        self.op("sync", "dma_start", out, in_, inc=16, **kw)

    def mm(self, out, lhsT, rhs, start=True, stop=True):
        self.op("tensor", "matmul", out, lhsT, rhs, start=start, stop=stop)

    def act(self, out, in_, func, **kw):
        self.op("scalar", "activation", out, in_, func, **kw)

    def tt(self, out, a, b, alu):
        self.op("vector", "tensor_tensor", out, a, b, alu)

    def tsc(self, out, in_, s1, s2, op0, op1=None):
        if op1 is None:
            self.op("vector", "tensor_scalar", out, in_, s1, None, op0)
        else:
            self.op("vector", "tensor_scalar", out, in_, s1, s2, op0, op1)

    def stt(self, out, in0, scalar, in1, op0, op1):
        self.op("vector", "scalar_tensor_tensor", out, in0, scalar, in1, op0, op1)

    def cp(self, out, in_):
        self.op("vector", "tensor_copy", out, in_)

    def build(self):
        plan = []
        cnt = 0
        cur = 0
        sem_final = {}
        for (eng, meth, args, kwargs, inc) in self.script:
            wait = (cur, cnt) if cnt > 0 else None
            if cnt + inc > K.SEM_SPAN:
                sem_final[cur] = cnt
                cur += 1
                cnt = 0
            plan.append((eng, meth, args, kwargs, inc, wait, (cur, cnt + inc)))
            cnt += inc
        sem_final[cur] = cnt
        self.final_sem = (cur, cnt)
        sems = [self.ctx.enter_context(self.nc.semaphore(f"gs{i}"))
                for i in range(cur + 1)]

        by_eng = {}
        for item in plan:
            by_eng.setdefault(item[0], []).append(item)

        with self.nc.Block() as block:
            def mk(engname):
                def f(eng):
                    for (e, meth, args, kwargs, inc, wait, after) in by_eng.get(engname, []):
                        if wait is not None:
                            ws, wv = wait
                            eng.wait_ge(sems[ws], wv)
                        inst = getattr(eng, meth)(*args, **kwargs)
                        inst.then_inc(sems[after[0]], inc)
                    if engname == "sync":
                        eng.wait_ge(sems[self.final_sem[0]], self.final_sem[1])
                return f
            if "sync" in by_eng or True:
                block.sync(mk("sync"))
            if "tensor" in by_eng:
                block.tensor(mk("tensor"))
            if "scalar" in by_eng:
                block.scalar(mk("scalar"))
            if "vector" in by_eng:
                block.vector(mk("vector"))
            if "gpsimd" in by_eng:
                block.gpsimd(mk("gpsimd"))
        return self.nc


def np_to_dt(d):
    d = np.dtype(d)
    if d == np.float32:
        return F32
    if d == np.int32:
        return I32
    if d == ml_dtypes.bfloat16:
        return BF16
    raise ValueError(d)


def c_ones_f32_col():
    return np.ones((128, 1), np.float32)

def c_ones_bf_col():
    return np.ones((128, 1), bf)

def c_ones_bf_row():
    return np.ones((1, 128), bf)

def c_ind_h2d(dt):
    m = np.zeros((16, 128), np.float32)
    for p in range(128):
        h = 2 * dt + (p // 64)
        m[h, p] = 1.0
    return m.astype(bf)

def c_ind_d2h(dt):
    return np.ascontiguousarray(c_ind_h2d(dt).T)


def emit_ln(k, xc, xn_out, sq, psA, psB, mrow, rrow, sA, sB, ones_col,
            psM, psR, xn_off=0, w=CH):
    for dt in range(NDT):
        k.tt(sq[0:128, 0:w], xc[dt][0:128, 0:w], xc[dt][0:128, 0:w], ALU.mult)
        k.mm(psA[0:1, 0:w], ones_col[0:128, 0:1], xc[dt][0:128, 0:w],
             start=(dt == 0), stop=(dt == NDT - 1))
        k.mm(psB[0:1, 0:w], ones_col[0:128, 0:1], sq[0:128, 0:w],
             start=(dt == 0), stop=(dt == NDT - 1))
    k.tsc(mrow[0:1, 0:w], psA[0:1, 0:w], 1.0 / D, None, ALU.mult)
    k.tsc(sB[0:1, 0:w], psB[0:1, 0:w], 1.0 / D, None, ALU.mult)
    k.tt(sA[0:1, 0:w], mrow[0:1, 0:w], mrow[0:1, 0:w], ALU.mult)
    k.tt(sB[0:1, 0:w], sB[0:1, 0:w], sA[0:1, 0:w], ALU.subtract)
    k.tsc(sB[0:1, 0:w], sB[0:1, 0:w], 1e-5, None, ALU.add)
    k.act(sA[0:1, 0:w], sB[0:1, 0:w], AF.Sqrt)
    k.op("vector", "reciprocal", rrow[0:1, 0:w], sA[0:1, 0:w])
    k.mm(psM[0:128, 0:w], ones_f32_row_g[0:1, 0:128], mrow[0:1, 0:w])
    k.mm(psR[0:128, 0:w], ones_f32_row_g[0:1, 0:128], rrow[0:1, 0:w])
    for dt in range(NDT):
        k.tt(sq[0:128, 0:w], xc[dt][0:128, 0:w], psM[0:128, 0:w], ALU.subtract)
        k.tt(xn_out[dt][0:128, xn_off:xn_off + w], sq[0:128, 0:w],
             psR[0:128, 0:w], ALU.mult)


ones_f32_row_g = None


def emit_wconv(k, wap, tiles, scratch, scale=None):
    R = wap.shape[0]
    C = wap.shape[1]
    r0 = 0
    i = 0
    while r0 < R:
        r = min(128, R - r0)
        k.dma(scratch[0:r, 0:C], wap[r0:r0 + r, 0:C])
        if scale is None:
            k.cp(tiles[i][0:r, 0:C], scratch[0:r, 0:C])
        else:
            k.tsc(tiles[i][0:r, 0:C], scratch[0:r, 0:C], scale, None, ALU.mult)
        r0 += r
        i += 1


def build_launch_A(weights):
    global ones_f32_row_g
    k = K()
    nc = k.nc
    xT = k.dram("xT", [D, S], F32, "ExternalInput")
    yT = k.dram("yT", [D, S], F32, "ExternalOutput")
    w_in = k.din("w_in", weights["W_in"])
    b_in = k.din("b_in", weights["b_in"])
    w_g = k.din("w_g", weights["W_g"])
    b_g = k.din("b_g", weights["b_g"])
    w_out = k.din("w_out", weights["W_out"])
    b_out = k.din("b_out", weights["b_out"])
    onc = k.din("onc", c_ones_f32_col())
    onr = k.din("onr", np.ones((1, 128), np.float32))

    E2 = 1032
    combD = k.dram("combD", [E2, S], BF16, "Internal")
    with k.ctx:
        ones_col = k.sb("ones_col", 128, 1, F32)
        ones_row = k.sb("ones_row", 1, 128, F32)
        ones_f32_row_g = ones_row
        k.dma(ones_col[0:128, 0:1], onc[0:128, 0:1])
        k.dma(ones_row[0:1, 0:128], onr[0:1, 0:128])

        wsc = k.sb("wsc", 128, 1032, F32)
        win_b = [k.sb(f"win{i}", 128, E2, BF16) for i in range(8)]
        emit_wconv(k, w_in, win_b, wsc)
        wg_b = [k.sb(f"wg{i}", 128, 8, BF16) for i in range(9)]
        emit_wconv(k, w_g, wg_b, wsc)
        wout_b = [k.sb(f"wout{i}", 128, D, BF16) for i in range(9)]
        emit_wconv(k, w_out, wout_b, wsc)
        bin_s = [k.sb(f"bin{i}", 128, 1, F32) for i in range(9)]
        for i in range(9):
            r = 128 if i < 8 else 8
            k.dma(bin_s[i][0:r, 0:1], b_in[i * 128:i * 128 + r, 0:1])
        bg_s = k.sb("bg", 8, 1, F32)
        k.dma(bg_s[0:8, 0:1], b_g[0:8, 0:1])
        bout_s = [k.sb(f"bout{i}", 128, 1, F32) for i in range(8)]
        for i in range(8):
            k.dma(bout_s[i][0:128, 0:1], b_out[i * 128:i * 128 + 128, 0:1])

        xc = [k.sb(f"xc{i}", 128, CH, F32) for i in range(NDT)]
        xn = [k.sb(f"xn{i}", 128, CH, BF16) for i in range(NDT)]
        comb = [k.sb(f"comb{i}", 128 if i < 8 else 8, CH, BF16) for i in range(9)]
        gS = k.sb("gS", 8, S, F32)
        siS = k.sb("siS", 8, S, F32)
        stS = k.sb("stS", 8, S, F32)
        st_bf = k.sb("st_bf", 8, S, BF16)
        sq = k.sb("sq", 128, CH, F32)
        mrow = k.sb("mrow", 1, CH, F32)
        rrow = k.sb("rrow", 1, CH, F32)
        sA = k.sb("sA", 1, CH, F32)
        sB = k.sb("sB", 1, CH, F32)
        yc = k.sb("yc", 128, CH, F32)

        psA = k.ps("psA", 1, CH)
        psB = k.ps("psB", 1, CH)
        psM = k.ps("psM", 128, CH)
        psR = k.ps("psR", 128, CH)
        psO = k.ps("psO", 128, CH)
        psG = k.ps("psG", 8, CH)

        for c in range(NCH):
            for dt in range(NDT):
                k.dma(xc[dt][0:128, 0:CH], xT[dt * 128:dt * 128 + 128, c * CH:(c + 1) * CH])
            emit_ln(k, xc, xn, sq, psA, psB, mrow, rrow, sA, sB, ones_col,
                    psM, psR)
            for ot in range(9):
                rows = 128 if ot < 8 else 8
                for dt in range(NDT):
                    k.mm(psO[0:rows, 0:CH],
                         win_b[dt][0:128, ot * 128:ot * 128 + rows],
                         xn[dt][0:128, 0:CH],
                         start=(dt == 0), stop=(dt == NDT - 1))
                k.tsc(comb[ot][0:rows, 0:CH], psO[0:rows, 0:CH],
                      bin_s[ot][0:rows, 0:1], None, ALU.add)
                k.dma(combD[ot * 128:ot * 128 + rows, c * CH:(c + 1) * CH],
                      comb[ot][0:rows, 0:CH])
            for kt in range(9):
                rows = 128 if kt < 8 else 8
                k.mm(psG[0:8, 0:CH],
                     wg_b[kt][0:rows, 0:8],
                     comb[kt][0:rows, 0:CH],
                     start=(kt == 0), stop=(kt == 8))
            k.act(gS[0:8, c * CH:(c + 1) * CH], psG[0:8, 0:CH], AF.Sigmoid,
                  bias=bg_s[0:8, 0:1])
            k.cp(siS[0:8, c * CH:(c + 1) * CH], comb[8][0:8, 0:CH])

        k.tsc(stS[0:8, 0:S], gS[0:8, 0:S], -1.0, 1.0, ALU.mult, ALU.add)
        k.tt(siS[0:8, 0:S], stS[0:8, 0:S], siS[0:8, 0:S], ALU.mult)
        k.op("vector", "tensor_tensor_scan", stS[0:8, 0:S], gS[0:8, 0:S],
             siS[0:8, 0:S], 0.0, ALU.mult, ALU.add)
        k.cp(st_bf[0:8, 0:S], stS[0:8, 0:S])

        for c in range(NCH):
            for kt in range(8):
                k.dma(comb[kt][0:128, 0:CH],
                      combD[kt * 128:kt * 128 + 128, c * CH:(c + 1) * CH])
            for dt in range(NDT):
                for kt in range(9):
                    if kt < 8:
                        rhs = comb[kt][0:128, 0:CH]
                        lhs = wout_b[kt][0:128, dt * 128:dt * 128 + 128]
                    else:
                        rhs = st_bf[0:8, c * CH:(c + 1) * CH]
                        lhs = wout_b[8][0:8, dt * 128:dt * 128 + 128]
                    k.mm(psO[0:128, 0:CH], lhs, rhs,
                         start=(kt == 0), stop=(kt == 8))
                k.dma(xc[0][0:128, 0:CH], xT[dt * 128:dt * 128 + 128, c * CH:(c + 1) * CH])
                k.stt(yc[0:128, 0:CH], psO[0:128, 0:CH], bout_s[dt][0:128, 0:1],
                      xc[0][0:128, 0:CH], ALU.add, ALU.add)
                k.dma(yT[dt * 128:dt * 128 + 128, c * CH:(c + 1) * CH], yc[0:128, 0:CH])

        k.build()
    return k


def build_launch_B(weights):
    global ones_f32_row_g
    k = K()
    H = 3072
    NHT = H // 128
    xT = k.dram("xT", [D, S], F32, "ExternalInput")
    yT = k.dram("yT", [D, S], F32, "ExternalOutput")
    hD = k.dram("hD", [H, S], BF16, "Internal")
    wgT = k.din("wgT", weights["wgT"])
    wuT = k.din("wuT", weights["wuT"])
    wdT = k.din("wdT", weights["wdT"])
    onc = k.din("onc", c_ones_f32_col())
    onr = k.din("onr", np.ones((1, 128), np.float32))
    SC = 1.0 / 127.0

    with k.ctx:
        ones_col = k.sb("ones_col", 128, 1, F32)
        ones_row = k.sb("ones_row", 1, 128, F32)
        ones_f32_row_g = ones_row
        k.dma(ones_col[0:128, 0:1], onc[0:128, 0:1])
        k.dma(ones_row[0:1, 0:128], onr[0:1, 0:128])

        wsc = k.sb("wsc", 128, H, I32)
        pool = [k.sb(f"wp{i}", 128, H, BF16) for i in range(16)]
        wg_b = pool[0:8]
        wu_b = pool[8:16]
        def wd_ap(j, c0, c1):
            return pool[j // 3][0:128, (j % 3) * D + c0:(j % 3) * D + c1]

        xc = [k.sb(f"xc{i}", 128, CH, F32) for i in range(NDT)]
        xn = [k.sb(f"xn{i}", 128, CH, BF16) for i in range(NDT)]
        hc = [k.sb(f"hc{i}", 128, CH, BF16) for i in range(NHT)]
        gt = k.sb("gt", 128, CH, BF16)
        sq = k.sb("sq", 128, CH, F32)
        mrow = k.sb("mrow", 1, CH, F32)
        rrow = k.sb("rrow", 1, CH, F32)
        sA = k.sb("sA", 1, CH, F32)
        sB = k.sb("sB", 1, CH, F32)
        yc = k.sb("yc", 128, CH, F32)

        psA = k.ps("psA", 1, CH)
        psB = k.ps("psB", 1, CH)
        psM = k.ps("psM", 128, CH)
        psR = k.ps("psR", 128, CH)
        psG = k.ps("psG", 128, CH)
        psU = k.ps("psU", 128, CH)
        psY = k.ps("psY", 128, CH)

        emit_wconv(k, wgT, wg_b, wsc, scale=SC)
        emit_wconv(k, wuT, wu_b, wsc, scale=SC)

        for c in range(NCH):
            for dt in range(NDT):
                k.dma(xc[dt][0:128, 0:CH], xT[dt * 128:dt * 128 + 128, c * CH:(c + 1) * CH])
            emit_ln(k, xc, xn, sq, psA, psB, mrow, rrow, sA, sB, ones_col, psM, psR)
            for ht in range(NHT):
                for dt in range(NDT):
                    k.mm(psG[0:128, 0:CH], wg_b[dt][0:128, ht * 128:ht * 128 + 128],
                         xn[dt][0:128, 0:CH], start=(dt == 0), stop=(dt == NDT - 1))
                for dt in range(NDT):
                    k.mm(psU[0:128, 0:CH], wu_b[dt][0:128, ht * 128:ht * 128 + 128],
                         xn[dt][0:128, 0:CH], start=(dt == 0), stop=(dt == NDT - 1))
                k.act(gt[0:128, 0:CH], psG[0:128, 0:CH], AF.Silu)
                k.tt(hc[ht][0:128, 0:CH], gt[0:128, 0:CH], psU[0:128, 0:CH], ALU.mult)
                k.dma(hD[ht * 128:ht * 128 + 128, c * CH:(c + 1) * CH], hc[ht][0:128, 0:CH])

        for j in range(NHT):
            k.dma(wsc[0:128, 0:D], wdT[j * 128:j * 128 + 128, 0:D])
            k.tsc(wd_ap(j, 0, D), wsc[0:128, 0:D], SC, None, ALU.mult)
        for c in range(NCH):
            for ht in range(NHT):
                k.dma(hc[ht][0:128, 0:CH], hD[ht * 128:ht * 128 + 128, c * CH:(c + 1) * CH])
            for dt in range(NDT):
                for ht in range(NHT):
                    k.mm(psY[0:128, 0:CH], wd_ap(ht, dt * 128, dt * 128 + 128),
                         hc[ht][0:128, 0:CH], start=(ht == 0), stop=(ht == NHT - 1))
                k.dma(xc[0][0:128, 0:CH], xT[dt * 128:dt * 128 + 128, c * CH:(c + 1) * CH])
                k.stt(yc[0:128, 0:CH], psY[0:128, 0:CH], 0.0,
                      xc[0][0:128, 0:CH], ALU.add, ALU.add)
                k.dma(yT[dt * 128:dt * 128 + 128, c * CH:(c + 1) * CH], yc[0:128, 0:CH])
        k.build()
    return k


def build_launch_C(weights):
    global ones_f32_row_g
    k = K()
    xT = k.dram("xT", [D, S], F32, "ExternalInput")
    qT = k.dram("qT", [D, S], BF16, "ExternalOutput")
    kT_ = k.dram("kT", [D, S], BF16, "ExternalOutput")
    uT = k.dram("uT", [D, S], BF16, "ExternalOutput")
    rT = k.dram("rT", [16, S], BF16, "ExternalOutput")
    wq = k.din("wq", weights["WqT"])
    wk = k.din("wk", weights["WkT"])
    wv = k.din("wv", weights["WvT"])
    ww = k.din("ww", weights["WwT"])
    wr = k.din("wr", weights["WrT"])
    bq = k.din("bq", weights["bq"])
    bk = k.din("bk", weights["bk"])
    bv = k.din("bv", weights["bv"])
    bw = k.din("bw", weights["bw"])
    br = k.din("br", weights["br"])
    indh = k.din("indh", np.concatenate([c_ind_h2d(dt) for dt in range(8)], axis=0))
    onc = k.din("onc", c_ones_f32_col())
    onr = k.din("onr", np.ones((1, 128), np.float32))

    with k.ctx:
        ones_col = k.sb("ones_col", 128, 1, F32)
        ones_row = k.sb("ones_row", 1, 128, F32)
        ones_f32_row_g = ones_row
        k.dma(ones_col[0:128, 0:1], onc[0:128, 0:1])
        k.dma(ones_row[0:1, 0:128], onr[0:1, 0:128])
        ind_b = [k.sb(f"ind{i}", 16, 128, BF16) for i in range(8)]
        for i in range(8):
            k.dma(ind_b[i][0:16, 0:128], indh[i * 16:i * 16 + 16, 0:128])

        wsc = k.sb("wsc", 128, D, F32)
        wq_b = [k.sb(f"wqb{i}", 128, D, BF16) for i in range(8)]
        wk_b = [k.sb(f"wkb{i}", 128, D, BF16) for i in range(8)]
        wv_b = [k.sb(f"wvb{i}", 128, D, BF16) for i in range(8)]
        ww_b = [k.sb(f"wwb{i}", 128, 16, BF16) for i in range(8)]
        wr_b = [k.sb(f"wrb{i}", 128, 16, BF16) for i in range(8)]
        emit_wconv(k, wq, wq_b, wsc)
        emit_wconv(k, wk, wk_b, wsc)
        emit_wconv(k, wv, wv_b, wsc)
        emit_wconv(k, ww, ww_b, wsc)
        emit_wconv(k, wr, wr_b, wsc)
        bq_s = [k.sb(f"bqs{i}", 128, 1, F32) for i in range(8)]
        bk_s = [k.sb(f"bks{i}", 128, 1, F32) for i in range(8)]
        bv_s = [k.sb(f"bvs{i}", 128, 1, F32) for i in range(8)]
        for i in range(8):
            k.dma(bq_s[i][0:128, 0:1], bq[i * 128:i * 128 + 128, 0:1])
            k.dma(bk_s[i][0:128, 0:1], bk[i * 128:i * 128 + 128, 0:1])
            k.dma(bv_s[i][0:128, 0:1], bv[i * 128:i * 128 + 128, 0:1])
        bw_s = k.sb("bws", 16, 1, F32)
        br_s = k.sb("brs", 16, 1, F32)
        k.dma(bw_s[0:16, 0:1], bw[0:16, 0:1])
        k.dma(br_s[0:16, 0:1], br[0:16, 0:1])

        xc = [k.sb(f"xc{i}", 128, CH, F32) for i in range(NDT)]
        xn = [k.sb(f"xn{i}", 128, CH, BF16) for i in range(NDT)]
        ob = k.sb("ob", 128, CH, BF16)
        vb = k.sb("vb", 128, CH, BF16)
        wrow = k.sb("wrow", 16, CH, BF16)
        sq = k.sb("sq", 128, CH, F32)
        mrow = k.sb("mrow", 1, CH, F32)
        rrow = k.sb("rrow", 1, CH, F32)
        sA = k.sb("sA", 1, CH, F32)
        sB = k.sb("sB", 1, CH, F32)

        psA = k.ps("psA", 1, CH)
        psB = k.ps("psB", 1, CH)
        psM = k.ps("psM", 128, CH)
        psR = k.ps("psR", 128, CH)
        psO = k.ps("psO", 128, CH)
        psW = k.ps("psW", 16, CH)

        for c in range(NCH):
            cs = slice(c * CH, (c + 1) * CH)
            for dt in range(NDT):
                k.dma(xc[dt][0:128, 0:CH], xT[dt * 128:dt * 128 + 128, cs])
            emit_ln(k, xc, xn, sq, psA, psB, mrow, rrow, sA, sB, ones_col, psM, psR)
            for (wb, bs, outd) in ((wq_b, bq_s, qT), (wk_b, bk_s, kT_)):
                for ot in range(8):
                    for dt in range(NDT):
                        k.mm(psO[0:128, 0:CH], wb[dt][0:128, ot * 128:ot * 128 + 128],
                             xn[dt][0:128, 0:CH], start=(dt == 0), stop=(dt == NDT - 1))
                    k.tsc(ob[0:128, 0:CH], psO[0:128, 0:CH], bs[ot][0:128, 0:1],
                          None, ALU.add)
                    k.dma(outd[ot * 128:ot * 128 + 128, cs], ob[0:128, 0:CH])
            for dt in range(NDT):
                k.mm(psW[0:16, 0:CH], ww_b[dt][0:128, 0:16], xn[dt][0:128, 0:CH],
                     start=(dt == 0), stop=(dt == NDT - 1))
            k.act(wrow[0:16, 0:CH], psW[0:16, 0:CH], AF.Sigmoid, bias=bw_s[0:16, 0:1])
            for dt in range(NDT):
                k.mm(psW[0:16, 0:CH], wr_b[dt][0:128, 0:16], xn[dt][0:128, 0:CH],
                     start=(dt == 0), stop=(dt == NDT - 1))
            k.act(ob[0:16, 0:CH], psW[0:16, 0:CH], AF.Sigmoid, bias=br_s[0:16, 0:1])
            k.dma(rT[0:16, cs], ob[0:16, 0:CH])
            for ot in range(8):
                for dt in range(NDT):
                    k.mm(psO[0:128, 0:CH], wv_b[dt][0:128, ot * 128:ot * 128 + 128],
                         xn[dt][0:128, 0:CH], start=(dt == 0), stop=(dt == NDT - 1))
                k.tsc(vb[0:128, 0:CH], psO[0:128, 0:CH], bv_s[ot][0:128, 0:1],
                      None, ALU.add)
                k.mm(psM[0:128, 0:CH], ind_b[ot][0:16, 0:128], wrow[0:16, 0:CH])
                k.tt(ob[0:128, 0:CH], vb[0:128, 0:CH], psM[0:128, 0:CH], ALU.mult)
                k.dma(uT[ot * 128:ot * 128 + 128, cs], ob[0:128, 0:CH])
        k.build()
    return k


def build_launch_D(weights):
    k = K()
    P = CH
    qA = k.dram("qA", [8 * D, P], BF16, "ExternalInput")
    kA = k.dram("kA", [8 * D, P], BF16, "ExternalInput")
    uA = k.dram("uA", [8 * D, P], BF16, "ExternalInput")
    rA = k.dram("rA", [128, P], BF16, "ExternalInput")
    xA = k.dram("xA", [8 * D, P], F32, "ExternalInput")
    yA = k.dram("yA", [8 * D, P], F32, "ExternalOutput")
    wo = k.din("wo", weights["WoT"])
    bo = k.din("bo", weights["bo"])
    indh = k.din("indh", np.concatenate([c_ind_h2d(dt) for dt in range(8)], axis=0))
    indt = k.din("indt", np.concatenate([c_ind_d2h(dt) for dt in range(8)], axis=1))

    with k.ctx:
        ind_b = [k.sb(f"ind{i}", 16, 128, BF16) for i in range(8)]
        indt_b = [k.sb(f"indt{i}", 128, 16, BF16) for i in range(8)]
        for i in range(8):
            k.dma(ind_b[i][0:16, 0:128], indh[i * 16:i * 16 + 16, 0:128])
            k.dma(indt_b[i][0:128, 0:16], indt[0:128, i * 16:i * 16 + 16])
        wsc = k.sb("wsc", 128, D, F32)
        wo_b = [k.sb(f"wob{i}", 128, D, BF16) for i in range(8)]
        emit_wconv(k, wo, wo_b, wsc)
        bo_s = [k.sb(f"bos{i}", 128, 1, F32) for i in range(8)]
        for i in range(8):
            k.dma(bo_s[i][0:128, 0:1], bo[i * 128:i * 128 + 128, 0:1])

        qb = [k.sb(f"qb{i}", 128, P, BF16) for i in range(8)]
        kb = [k.sb(f"kb{i}", 128, P, BF16) for i in range(8)]
        ub = [k.sb(f"ub{i}", 128, P, BF16) for i in range(8)]
        rb = k.sb("rb", 16, P, BF16)
        mb = [k.sb(f"mb{i}", 128, P, BF16) for i in range(8)]
        acc = [k.sb(f"acc{i}", 128, P, F32) for i in range(8)]
        gt = [k.sb(f"gt{i}", 128, P, BF16) for i in range(8)]
        scb = k.sb("scb", 16, P, BF16)
        tmp = k.sb("tmp", 128, P, BF16)
        xcd = k.sb("xcd", 128, P, F32)
        yc = k.sb("yc", 128, P, F32)

        psS = k.ps("psS", 16, P)
        psB_ = k.ps("psB_", 128, P)
        psY = k.ps("psY", 128, P)

        for b in range(8):
            for dt in range(8):
                k.dma(qb[dt][0:128, 0:P], qA[b * D + dt * 128:b * D + dt * 128 + 128, 0:P])
            k.dma(rb[0:16, 0:P], rA[b * 16:b * 16 + 16, 0:P])
            for sp in range(8):
                for dt in range(8):
                    k.dma(kb[dt][0:128, 0:P], kA[sp * D + dt * 128:sp * D + dt * 128 + 128, 0:P])
                    k.dma(ub[dt][0:128, 0:P], uA[sp * D + dt * 128:sp * D + dt * 128 + 128, 0:P])
                for dt in range(8):
                    k.tt(mb[dt][0:128, 0:P], qb[dt][0:128, 0:P], kb[dt][0:128, 0:P], ALU.mult)
                for dt in range(8):
                    k.mm(psS[0:16, 0:P], indt_b[dt][0:128, 0:16], mb[dt][0:128, 0:P],
                         start=(dt == 0), stop=(dt == 7))
                k.cp(scb[0:16, 0:P], psS[0:16, 0:P])
                for dt in range(8):
                    k.mm(psB_[0:128, 0:P], ind_b[dt][0:16, 0:128], scb[0:16, 0:P])
                    k.tt(tmp[0:128, 0:P], ub[dt][0:128, 0:P], psB_[0:128, 0:P], ALU.mult)
                    if sp == 0:
                        k.cp(acc[dt][0:128, 0:P], tmp[0:128, 0:P])
                    else:
                        k.tt(acc[dt][0:128, 0:P], acc[dt][0:128, 0:P], tmp[0:128, 0:P], ALU.add)
            for dt in range(8):
                k.mm(psB_[0:128, 0:P], ind_b[dt][0:16, 0:128], rb[0:16, 0:P])
                k.tt(gt[dt][0:128, 0:P], acc[dt][0:128, 0:P], psB_[0:128, 0:P], ALU.mult)
            for dt in range(8):
                for kt in range(8):
                    k.mm(psY[0:128, 0:P], wo_b[kt][0:128, dt * 128:dt * 128 + 128],
                         gt[kt][0:128, 0:P], start=(kt == 0), stop=(kt == 7))
                k.dma(xcd[0:128, 0:P], xA[b * D + dt * 128:b * D + dt * 128 + 128, 0:P])
                k.stt(yc[0:128, 0:P], psY[0:128, 0:P], bo_s[dt][0:128, 0:1],
                      xcd[0:128, 0:P], ALU.add, ALU.add)
                k.dma(yA[b * D + dt * 128:b * D + dt * 128 + 128, 0:P], yc[0:128, 0:P])
        k.build()
    return k


def c_masks_F():
    mneg = np.zeros((6, 128, CH), np.float32)
    mbin = np.zeros((6, 128, CH), np.float32)
    for j in range(6):
        kr = np.arange(128)[:, None]
        qr = np.arange(CH)[None, :]
        delta = 128 * (j - 1) + kr - qr
        inw = (delta >= -128) & (delta <= 127)
        mneg[j] = np.where(inw, 0.0, -30000.0)
        mbin[j] = inw.astype(np.float32)
    return mneg, mbin


def c_wcm_F():
    i = np.arange(S)
    lo = np.maximum(0, i - 128)
    hi = np.minimum(S - 1, i + 127)
    cnt = hi - lo + 1
    return (S - cnt).astype(np.float32).reshape(1, S)


def build_launch_F(weights):
    global ones_f32_row_g
    k = K()
    xT = k.dram("xT", [D, S], F32, "ExternalInput")
    yT = k.dram("yT", [D, S], F32, "ExternalOutput")
    qD = k.dram("qD", [D, S], BF16, "Internal")
    kD = k.dram("kD", [D, S], BF16, "Internal")
    vD = k.dram("vD", [S, D], BF16, "Internal")
    wq = k.din("wq", weights["WqT"])
    wk = k.din("wk", weights["WkT"])
    wv = k.din("wv", weights["WvT"])
    bq = k.din("bq", weights["bq"])
    bk = k.din("bk", weights["bk"])
    bvr = k.din("bvr", weights["bv_row"])
    mnegA, mbinA = c_masks_F()
    mnegI = k.din("mnegI", mnegA.reshape(6 * 128, CH))
    mbinI = k.din("mbinI", mbinA.reshape(6 * 128, CH).astype(bf))
    wcmI = k.din("wcmI", c_wcm_F())
    onc = k.din("onc", c_ones_f32_col())
    onr = k.din("onr", np.ones((1, 128), np.float32))
    oncb = k.din("oncb", np.ones((128, 1), bf))
    onrb = k.din("onrb", np.ones((1, 512), bf))

    with k.ctx:
        ones_col = k.sb("ones_col", 128, 1, F32)
        ones_row = k.sb("ones_row", 1, 128, F32)
        ones_f32_row_g = ones_row
        onesb_col = k.sb("onesb_col", 128, 1, BF16)
        onesb_row = k.sb("onesb_row", 1, 512, BF16)
        k.dma(ones_col[0:128, 0:1], onc[0:128, 0:1])
        k.dma(ones_row[0:1, 0:128], onr[0:1, 0:128])
        k.dma(onesb_col[0:128, 0:1], oncb[0:128, 0:1])
        k.dma(onesb_row[0:1, 0:512], onrb[0:1, 0:512])
        mneg_b = [k.sb(f"mneg{j}", 128, CH, F32) for j in range(6)]
        mbin_b = [k.sb(f"mbin{j}", 128, CH, BF16) for j in range(6)]
        for j in range(6):
            k.dma(mneg_b[j][0:128, 0:CH], mnegI[j * 128:j * 128 + 128, 0:CH])
            k.dma(mbin_b[j][0:128, 0:CH], mbinI[j * 128:j * 128 + 128, 0:CH])
        wcm = k.sb("wcm", 1, S, F32)
        k.dma(wcm[0:1, 0:S], wcmI[0:1, 0:S])

        wsc = k.sb("wsc", 128, D, F32)
        wq_b = [k.sb(f"wqb{i}", 128, D, BF16) for i in range(8)]
        wk_b = [k.sb(f"wkb{i}", 128, D, BF16) for i in range(8)]
        wv_b = [k.sb(f"wvb{i}", 128, D, BF16) for i in range(8)]
        emit_wconv(k, wq, wq_b, wsc)
        emit_wconv(k, wk, wk_b, wsc)
        emit_wconv(k, wv, wv_b, wsc)
        for i in range(8):
            k.tsc(wq_b[i][0:128, 0:D], wq_b[i][0:128, 0:D], 0.125, None, ALU.mult)
        bq_s = [k.sb(f"bqs{i}", 128, 1, F32) for i in range(8)]
        bk_s = [k.sb(f"bks{i}", 128, 1, F32) for i in range(8)]
        for i in range(8):
            k.dma(bq_s[i][0:128, 0:1], bq[i * 128:i * 128 + 128, 0:1])
            k.tsc(bq_s[i][0:128, 0:1], bq_s[i][0:128, 0:1], 0.125, None, ALU.mult)
            k.dma(bk_s[i][0:128, 0:1], bk[i * 128:i * 128 + 128, 0:1])
        bv_row = k.sb("bv_row", 1, D, F32)
        k.dma(bv_row[0:1, 0:D], bvr[0:1, 0:D])
        bv_rb = k.sb("bv_rb", 1, D, BF16)
        k.cp(bv_rb[0:1, 0:D], bv_row[0:1, 0:D])

        xc = [k.sb(f"xc{i}", 128, CH, F32) for i in range(NDT)]
        xn = [k.sb(f"xn{i}", 128, CH, BF16) for i in range(NDT)]
        ob = k.sb("ob", 128, CH, BF16)
        sq = k.sb("sq", 128, CH, F32)
        mrow = k.sb("mrow", 1, CH, F32)
        rrow = k.sb("rrow", 1, CH, F32)
        sA = k.sb("sA", 1, CH, F32)
        sB = k.sb("sB", 1, CH, F32)
        vt_row = k.sb("vt_row", 1, D, F32)
        vt_rb = k.sb("vt_rb", 1, D, BF16)

        psA = k.ps("psA", 1, CH)
        psB = k.ps("psB", 1, CH)
        psM = k.ps("psM", 128, CH)
        psR = k.ps("psR", 128, CH)
        psO = k.ps("psO", 128, CH)
        psVT = k.ps("psVT", 1, D)

        for c in range(NCH):
            cs = slice(c * CH, (c + 1) * CH)
            for dt in range(NDT):
                k.dma(xc[dt][0:128, 0:CH], xT[dt * 128:dt * 128 + 128, cs])
            emit_ln(k, xc, xn, sq, psA, psB, mrow, rrow, sA, sB, ones_col, psM, psR)
            for (wb, bs, outd) in ((wq_b, bq_s, qD), (wk_b, bk_s, kD)):
                for ot in range(8):
                    for dt in range(NDT):
                        k.mm(psO[0:128, 0:CH], wb[dt][0:128, ot * 128:ot * 128 + 128],
                             xn[dt][0:128, 0:CH], start=(dt == 0), stop=(dt == NDT - 1))
                    k.tsc(ob[0:128, 0:CH], psO[0:128, 0:CH], bs[ot][0:128, 0:1],
                          None, ALU.add)
                    k.dma(outd[ot * 128:ot * 128 + 128, cs], ob[0:128, 0:CH])
            for ss in range(4):
                for oh in range(2):
                    for dt in range(NDT):
                        k.mm(psO[0:128, 0:CH], xn[dt][0:128, ss * 128:ss * 128 + 128],
                             wv_b[dt][0:128, oh * 512:oh * 512 + 512],
                             start=(dt == 0), stop=False)
                    k.mm(psO[0:128, 0:CH],
                         onesb_row[0:1, 0:128],
                         bv_rb[0:1, oh * 512:oh * 512 + 512], start=False, stop=True)
                    k.cp(ob[0:128, 0:CH], psO[0:128, 0:CH])
                    k.dma(vD[c * CH + ss * 128:c * CH + ss * 128 + 128,
                            oh * 512:oh * 512 + 512], ob[0:128, 0:CH])

        qb = [k.sb(f"qb{i}", 128, CH, BF16) for i in range(8)]
        kw = [k.sb(f"kw{i}", 128, 6 * 128, BF16) for i in range(8)]
        vw = [k.sb(f"vw{i}", 128, D, BF16) for i in range(6)]
        oT = [k.sb(f"oT{i}", 128, CH, BF16) for i in range(8)]
        fbf = k.sb("fbf", 128, CH, BF16)
        tf32 = k.sb("tf32", 128, CH, F32)
        dd = k.sb("dd", 1, CH, F32)
        rd = k.sb("rd", 1, CH, F32)
        rdb = k.sb("rdb", 1, CH, BF16)
        obf = k.sb("obf", 64, CH, F32)
        psS2, psO2, psD2, psRB = psM, psO, psA, psR

        for kt in range(32):
            k.dma(vw[0][0:128, 0:D], vD[kt * 128:kt * 128 + 128, 0:D])
            for oh in range(2):
                k.mm(psVT[0:1, oh * 512:oh * 512 + 512],
                     onesb_col[0:128, 0:1], vw[0][0:128, oh * 512:oh * 512 + 512],
                     start=(kt == 0 and oh == 0), stop=(kt == 31 and oh == 1))
        k.cp(vt_row[0:1, 0:D], psVT[0:1, 0:D])
        k.cp(vt_rb[0:1, 0:D], vt_row[0:1, 0:D])

        for c in range(NCH):
            cs = slice(c * CH, (c + 1) * CH)
            kt0 = max(0, 4 * c - 1)
            kt1 = min(32, 4 * c + 5)
            nkt = kt1 - kt0
            for dt in range(NDT):
                k.dma(qb[dt][0:128, 0:CH], qD[dt * 128:dt * 128 + 128, cs])
                k.dma(kw[dt][0:128, 0:nkt * 128],
                      kD[dt * 128:dt * 128 + 128, kt0 * 128:kt1 * 128])
            for j in range(nkt):
                k.dma(vw[j][0:128, 0:D], vD[(kt0 + j) * 128:(kt0 + j) * 128 + 128, 0:D])
            for h in range(16):
                hp = (h % 2) * 64
                hd = h // 2
                for j in range(nkt):
                    mi = (kt0 + j) - (4 * c - 1)
                    k.mm(psS2[0:128, 0:CH],
                         kw[hd][hp:hp + 64, j * 128:j * 128 + 128],
                         qb[hd][hp:hp + 64, 0:CH], start=True, stop=True)
                    k.tt(tf32[0:128, 0:CH], psS2[0:128, 0:CH], mneg_b[mi][0:128, 0:CH],
                         ALU.add)
                    k.act(fbf[0:128, 0:CH], tf32[0:128, 0:CH], AF.Exp)
                    k.tt(fbf[0:128, 0:CH], fbf[0:128, 0:CH], mbin_b[mi][0:128, 0:CH],
                         ALU.subtract)
                    k.mm(psO2[0:64, 0:CH], vw[j][0:128, h * 64:h * 64 + 64],
                         fbf[0:128, 0:CH], start=(j == 0), stop=False)
                    k.mm(psD2[0:1, 0:CH], onesb_col[0:128, 0:1], fbf[0:128, 0:CH],
                         start=(j == 0), stop=(j == nkt - 1))
                k.mm(psO2[0:64, 0:CH],
                     vt_rb[0:1, h * 64:h * 64 + 64], onesb_row[0:1, 0:CH],
                     start=False, stop=True)
                k.tt(dd[0:1, 0:CH], psD2[0:1, 0:CH], wcm[0:1, cs], ALU.add)
                k.op("vector", "reciprocal", rd[0:1, 0:CH], dd[0:1, 0:CH])
                k.cp(rdb[0:1, 0:CH], rd[0:1, 0:CH])
                k.mm(psRB[0:64, 0:CH], onesb_row[0:1, 0:64],
                     rdb[0:1, 0:CH], start=True, stop=True)
                k.cp(obf[0:64, 0:CH], psO2[0:64, 0:CH])
                k.tt(oT[hd][hp:hp + 64, 0:CH], obf[0:64, 0:CH], psRB[0:64, 0:CH],
                     ALU.mult)
            for dt in range(NDT):
                k.dma(xc[0][0:128, 0:CH], xT[dt * 128:dt * 128 + 128, cs])
                k.tt(sq[0:128, 0:CH], xc[0][0:128, 0:CH], oT[dt][0:128, 0:CH], ALU.add)
                k.dma(yT[dt * 128:dt * 128 + 128, cs], sq[0:128, 0:CH])
        k.build()
    return k


def build_launch_W():
    k = K()
    FF = 4096
    w1e = k.dram("w1e", [D, FF], F32, "ExternalInput")
    w2re = k.dram("w2re", [8 * FF, 128], F32, "ExternalInput")
    w1b = k.dram("w1b", [D, FF], BF16, "ExternalOutput")
    w2b = k.dram("w2b", [8 * 128, FF], BF16, "ExternalOutput")
    with k.ctx:
        sc = k.sb("sc", 128, FF, F32)
        cb = k.sb("cb", 128, FF, BF16)
        for dt in range(8):
            k.dma(sc[0:128, 0:FF], w1e[dt * 128:dt * 128 + 128, 0:FF])
            k.cp(cb[0:128, 0:FF], sc[0:128, 0:FF])
            k.dma(w1b[dt * 128:dt * 128 + 128, 0:FF], cb[0:128, 0:FF])
        for dt in range(8):
            for ft in range(32):
                k.dma(sc[0:128, ft * 128:ft * 128 + 128],
                      w2re[dt * FF + ft * 128:dt * FF + ft * 128 + 128, 0:128])
            k.cp(cb[0:128, 0:FF], sc[0:128, 0:FF])
            k.dma(w2b[dt * 128:(dt + 1) * 128, 0:FF], cb[0:128, 0:FF])
        k.build()
    return k


def build_launch_G(weights):
    global ones_f32_row_g
    k = K()
    FF = 4096
    NFT = 32
    xT = k.dram("xT", [D, S], F32, "ExternalInput")
    yT = k.dram("yT", [D, S], F32, "ExternalOutput")
    xnD = k.dram("xnD", [D, S], BF16, "Internal")
    wD = k.dram("wD", [8, S], BF16, "Internal")
    yeD = k.dram("yeD", [8 * D, S], BF16, "Internal")
    w1ball = k.dram("w1ball", [8 * D, FF], BF16, "ExternalInput")
    w2ball = k.dram("w2ball", [8 * D, FF], BF16, "ExternalInput")
    b1T = k.din("b1T", weights["b1T"])
    b2r = k.din("b2r", weights["b2"])
    e01I = k.din("e01I", np.array([[1.0]] * 2 + [[0.0]] * 126, np.float32))
    onc = k.din("onc", c_ones_f32_col())
    onr = k.din("onr", np.ones((1, 128), np.float32))
    onrb = k.din("onrb", np.ones((1, 512), bf))

    with k.ctx:
        ones_col = k.sb("ones_col", 128, 1, F32)
        ones_row = k.sb("ones_row", 1, 128, F32)
        ones_f32_row_g = ones_row
        onesb_row = k.sb("onesb_row", 1, 512, BF16)
        e01 = k.sb("e01", 128, 1, F32)
        k.dma(ones_col[0:128, 0:1], onc[0:128, 0:1])
        k.dma(ones_row[0:1, 0:128], onr[0:1, 0:128])
        k.dma(onesb_row[0:1, 0:512], onrb[0:1, 0:512])
        k.dma(e01[0:128, 0:1], e01I[0:128, 0:1])

        CP = 256
        b1s = [k.sb(f"b1s{i}", 128, 8, F32) for i in range(NFT)]
        for i in range(NFT):
            k.dma(b1s[i][0:128, 0:8], b1T[i * 128:i * 128 + 128, 0:8])
        b2one = k.sb("b2one", 1, D, BF16)
        b2f = k.sb("b2f", 1, D, F32)

        xc = [k.sb(f"xc{i}", 128, CP, F32) for i in range(NDT)]
        xn = [k.sb(f"xn{i}", 128, CH, BF16) for i in range(NDT)]
        hc = [k.sb(f"hc{i}", 128, CH, BF16) for i in range(NFT)]
        w1s = [k.sb(f"w1s{i}", 128, FF, BF16) for i in range(8)]
        w2s = [k.sb(f"w2s{i}", 128, FF, BF16) for i in range(8)]
        sq = k.sb("sq", 128, CP, F32)
        mrow = k.sb("mrow", 1, CP, F32)
        rrow = k.sb("rrow", 1, CP, F32)
        sA = k.sb("sA", 1, CP, F32)
        sB = k.sb("sB", 1, CP, F32)
        trow = k.sb("trow", 1, CP, F32)
        t1r = k.sb("t1r", 1, CP, F32)
        t2r = k.sb("t2r", 1, CP, F32)
        sgr = k.sb("sgr", 1, CP, F32)
        irow = k.sb("irow", 1, CP, I32)
        wrow = k.sb("wrow", 1, CH, BF16)
        wbb = k.sb("wbb", 128, CH, BF16)
        yc = k.sb("yc", 128, CH, BF16)
        ycf = k.sb("ycf", 128, CH, F32)

        psA = k.ps("psA", 1, CH)
        psB = k.ps("psB", 1, CH)
        psM = k.ps("psM", 128, CH)
        psR = k.ps("psR", 128, CH)
        psH = k.ps("psH", 128, CH)
        psY = k.ps("psY", 128, CH)

        for c in range(S // CP):
            cs = slice(c * CP, (c + 1) * CP)
            for dt in range(NDT):
                k.dma(xc[dt][0:128, 0:CP], xT[dt * 128:dt * 128 + 128, cs])
            emit_ln(k, xc, xn, sq, psA, psB, mrow, rrow, sA, sB, ones_col,
                    psM, psR, w=CP)
            for dt in range(NDT):
                k.dma(xnD[dt * 128:dt * 128 + 128, cs], xn[dt][0:128, 0:CP])
            k.mm(psA[0:1, 0:CP], e01[0:128, 0:1], xc[0][0:128, 0:CP])
            k.stt(trow[0:1, 0:CP], mrow[0:1, 0:CP], -2.0, psA[0:1, 0:CP],
                  ALU.mult, ALU.add)
            k.tt(trow[0:1, 0:CP], trow[0:1, 0:CP], rrow[0:1, 0:CP], ALU.mult)
            k.cp(irow[0:1, 0:CP], trow[0:1, 0:CP])
            k.cp(t1r[0:1, 0:CP], irow[0:1, 0:CP])
            k.tt(t2r[0:1, 0:CP], t1r[0:1, 0:CP], trow[0:1, 0:CP], ALU.is_gt)
            k.tsc(sgr[0:1, 0:CP], trow[0:1, 0:CP], 0.0, None, ALU.is_ge)
            k.tt(t2r[0:1, 0:CP], t2r[0:1, 0:CP], sgr[0:1, 0:CP], ALU.mult)
            k.tt(t1r[0:1, 0:CP], t1r[0:1, 0:CP], t2r[0:1, 0:CP], ALU.subtract)
            k.tt(t2r[0:1, 0:CP], t1r[0:1, 0:CP], trow[0:1, 0:CP], ALU.is_lt)
            k.tsc(sgr[0:1, 0:CP], sgr[0:1, 0:CP], -1.0, 1.0, ALU.mult, ALU.add)
            k.tt(t2r[0:1, 0:CP], t2r[0:1, 0:CP], sgr[0:1, 0:CP], ALU.mult)
            k.tt(t1r[0:1, 0:CP], t1r[0:1, 0:CP], t2r[0:1, 0:CP], ALU.add)
            k.cp(irow[0:1, 0:CP], t1r[0:1, 0:CP])
            k.tsc(irow[0:1, 0:CP], irow[0:1, 0:CP], 7, None, ALU.bitwise_and)
            k.cp(trow[0:1, 0:CP], irow[0:1, 0:CP])
            for e in range(8):
                k.tsc(t1r[0:1, 0:CP], trow[0:1, 0:CP], float(e), None, ALU.is_equal)
                k.tsc(t2r[0:1, 0:CP], trow[0:1, 0:CP], float((e - 1) % 8), None,
                      ALU.is_equal)
                k.tt(t1r[0:1, 0:CP], t1r[0:1, 0:CP], t2r[0:1, 0:CP], ALU.add)
                k.tsc(wrow[0:1, 0:CP], t1r[0:1, 0:CP], 0.5, None, ALU.mult)
                k.dma(wD[e:e + 1, cs], wrow[0:1, 0:CP])

        for e in range(8):
            for dt in range(8):
                k.dma(w1s[dt][0:128, 0:FF], w1ball[e * D + dt * 128:e * D + dt * 128 + 128, 0:FF])
                k.dma(w2s[dt][0:128, 0:FF], w2ball[e * D + dt * 128:e * D + dt * 128 + 128, 0:FF])
            k.dma(b2f[0:1, 0:D], b2r[e:e + 1, 0:D])
            k.cp(b2one[0:1, 0:D], b2f[0:1, 0:D])
            for c in range(NCH):
                cs = slice(c * CH, (c + 1) * CH)
                for dt in range(NDT):
                    k.dma(xn[dt][0:128, 0:CH], xnD[dt * 128:dt * 128 + 128, cs])
                k.dma(wrow[0:1, 0:CH], wD[e:e + 1, cs])
                k.mm(psM[0:128, 0:CH], onesb_row[0:1, 0:128], wrow[0:1, 0:CH])
                k.cp(wbb[0:128, 0:CH], psM[0:128, 0:CH])
                for ft in range(NFT):
                    for dt in range(NDT):
                        k.mm(psH[0:128, 0:CH], w1s[dt][0:128, ft * 128:ft * 128 + 128],
                             xn[dt][0:128, 0:CH], start=(dt == 0), stop=(dt == NDT - 1))
                    k.act(hc[ft][0:128, 0:CH], psH[0:128, 0:CH], AF.Relu,
                          bias=b1s[ft][0:128, e:e + 1])
                for dt in range(NDT):
                    for ft in range(NFT):
                        k.mm(psY[0:128, 0:CH],
                             w2s[dt][0:128, ft * 128:ft * 128 + 128],
                             hc[ft][0:128, 0:CH], start=(ft == 0), stop=False)
                    k.mm(psY[0:128, 0:CH],
                         b2one[0:1, dt * 128:dt * 128 + 128], onesb_row[0:1, 0:CH],
                         start=False, stop=True)
                    k.tt(yc[0:128, 0:CH], psY[0:128, 0:CH], wbb[0:128, 0:CH],
                         ALU.mult)
                    k.dma(yeD[e * D + dt * 128:e * D + dt * 128 + 128, cs],
                          yc[0:128, 0:CH])

        for c in range(NCH):
            cs = slice(c * CH, (c + 1) * CH)
            for dt in range(NDT):
                k.dma(ycf[0:128, 0:CH], xT[dt * 128:dt * 128 + 128, cs])
                for e in range(8):
                    k.dma(yc[0:128, 0:CH], yeD[e * D + dt * 128:e * D + dt * 128 + 128, cs])
                    k.tt(ycf[0:128, 0:CH], ycf[0:128, 0:CH], yc[0:128, 0:CH], ALU.add)
                k.dma(yT[dt * 128:dt * 128 + 128, cs], ycf[0:128, 0:CH])
        k.build()
    return k


def c_rope():
    HD = 64
    inv = (10000.0 ** (-np.arange(0, HD, 2, dtype=np.float64) / HD))
    ang = np.arange(S, dtype=np.float64)[:, None] * inv[None, :]
    cosv = np.cos(ang)
    sinv = np.sin(ang)
    C = np.zeros((128, S), np.float32)
    Sg = np.zeros((128, S), np.float32)
    for p in range(128):
        idx = p % 32
        sign = 1.0 if (p % 64) < 32 else -1.0
        C[p] = cosv[:, idx]
        Sg[p] = sign * sinv[:, idx]
    return C.astype(bf), Sg.astype(bf)


def c_perm():
    P = np.zeros((128, 128), np.float32)
    for i in range(128):
        sw = i + 32 if (i % 64) < 32 else i - 32
        P[sw, i] = 1.0
    return P.astype(bf)


def c_masks_H():
    m = np.zeros((4, 128, CH), np.float32)
    for j in range(4):
        kr = np.arange(128)[:, None]
        qr = np.arange(CH)[None, :]
        m[j] = np.where(128 * j + kr <= qr, 0.0, -30000.0)
    return m


def c_kvind(t):
    m = np.zeros((128, 4), np.float32)
    for p in range(128):
        g = 2 * t + (p // 64)
        m[p, g] = 1.0
    return m.astype(bf)


def build_launch_H(weights):
    global ones_f32_row_g
    k = K()
    KV = 256
    xT = k.dram("xT", [D, S], F32, "ExternalInput")
    yT = k.dram("yT", [D, S], F32, "ExternalOutput")
    qD = k.dram("qD", [D, S], BF16, "Internal")
    wq = k.din("wq", weights["WqT"])
    wk = k.din("wk", weights["WkT"])
    wv = k.din("wv", weights["WvT"])
    wp = k.din("wp", weights["WprojT"])
    qgc = k.din("qgc", weights["qg"])
    qgr = k.din("qgr", weights["qg_row"])
    CroI, SroI = c_rope()
    Cro = k.din("Cro", CroI)
    Sro = k.din("Sro", SroI)
    permI = k.din("permI", c_perm())
    mnegI = k.din("mnegI", c_masks_H().reshape(4 * 128, CH))
    indh = k.din("indh", np.concatenate([c_ind_h2d(dt) for dt in range(8)], axis=0))
    indt = k.din("indt", np.concatenate([c_ind_d2h(dt) for dt in range(8)], axis=1))
    kvh2dI = k.din("kvh2dI", np.concatenate(
        [np.ascontiguousarray(c_kvind(t).T) for t in range(2)], axis=0))
    kvd2hI = k.din("kvd2hI", np.concatenate([c_kvind(t) for t in range(2)], axis=1))
    onc = k.din("onc", c_ones_f32_col())
    onr = k.din("onr", np.ones((1, 128), np.float32))
    oncb = k.din("oncb", np.ones((128, 1), bf))
    onrb = k.din("onrb", np.ones((1, 512), bf))
    EPS = 1.1920929e-07

    with k.ctx:
        ones_col = k.sb("ones_col", 128, 1, F32)
        ones_row = k.sb("ones_row", 1, 128, F32)
        ones_f32_row_g = ones_row
        onesb_col = k.sb("onesb_col", 128, 1, BF16)
        onesb_row = k.sb("onesb_row", 1, 512, BF16)
        k.dma(ones_col[0:128, 0:1], onc[0:128, 0:1])
        k.dma(ones_row[0:1, 0:128], onr[0:1, 0:128])
        k.dma(onesb_col[0:128, 0:1], oncb[0:128, 0:1])
        k.dma(onesb_row[0:1, 0:512], onrb[0:1, 0:512])
        Ct = k.sb("Ct", 128, S, BF16)
        St = k.sb("St", 128, S, BF16)
        k.dma(Ct[0:128, 0:S], Cro[0:128, 0:S])
        k.dma(St[0:128, 0:S], Sro[0:128, 0:S])
        perm_b = k.sb("perm_b", 128, 128, BF16)
        k.dma(perm_b[0:128, 0:128], permI[0:128, 0:128])
        mneg_b = [k.sb(f"mneg{j}", 128, CH, F32) for j in range(4)]
        for j in range(4):
            k.dma(mneg_b[j][0:128, 0:CH], mnegI[j * 128:j * 128 + 128, 0:CH])
        ind_b = [k.sb(f"ind{i}", 16, 128, BF16) for i in range(8)]
        indt_b = [k.sb(f"indt{i}", 128, 16, BF16) for i in range(8)]
        for i in range(8):
            k.dma(ind_b[i][0:16, 0:128], indh[i * 16:i * 16 + 16, 0:128])
            k.dma(indt_b[i][0:128, 0:16], indt[0:128, i * 16:i * 16 + 16])
        kvh2d = [k.sb(f"kvh2d{i}", 4, 128, BF16) for i in range(2)]
        kvd2h = [k.sb(f"kvd2h{i}", 128, 4, BF16) for i in range(2)]
        for i in range(2):
            k.dma(kvh2d[i][0:4, 0:128], kvh2dI[i * 4:i * 4 + 4, 0:128])
            k.dma(kvd2h[i][0:128, 0:4], kvd2hI[0:128, i * 4:i * 4 + 4])

        wsc = k.sb("wsc", 128, D, F32)
        wq_b = [k.sb(f"wqb{i}", 128, D, BF16) for i in range(8)]
        wk_b = [k.sb(f"wkb{i}", 128, KV, BF16) for i in range(8)]
        wv_b = [k.sb(f"wvb{i}", 128, KV, BF16) for i in range(8)]
        wp_b = [k.sb(f"wpb{i}", 128, D, BF16) for i in range(8)]
        emit_wconv(k, wq, wq_b, wsc)
        emit_wconv(k, wk, wk_b, wsc)
        emit_wconv(k, wv, wv_b, wsc)
        emit_wconv(k, wp, wp_b, wsc)

        qg8 = k.sb("qg8", 16, 1, F32)
        k.dma(qg8[0:16, 0:1], qgc[0:16, 0:1])
        k.tsc(qg8[0:16, 0:1], qg8[0:16, 0:1], 0.125, None, ALU.mult)
        qga = k.sb("qga", 1, 16, F32)
        k.dma(qga[0:1, 0:16], qgr[0:1, 0:16])
        k.act(qga[0:1, 0:16], qga[0:1, 0:16], AF.Square)
        k.act(qga[0:1, 0:16], qga[0:1, 0:16], AF.Sqrt)
        k.tsc(qga[0:1, 0:16], qga[0:1, 0:16], -8.0, None, ALU.mult)
        biasc = k.sb("biasc", 128, 16, F32)

        xc = [k.sb(f"xc{i}", 128, CH, F32) for i in range(NDT)]
        xn = [k.sb(f"xn{i}", 128, CH, BF16) for i in range(NDT)]
        qt = [k.sb(f"qt{i}", 128, CH, BF16) for i in range(NDT)]
        Kt = [k.sb(f"Kt{i}", 128, S, BF16) for i in range(2)]
        Vt = [k.sb(f"Vt{i}", 128, KV, BF16) for i in range(32)]
        oT = [k.sb(f"oT{i}", 128, CH, BF16) for i in range(NDT)]
        qh = k.sb("qh", 128, CH, BF16)
        sq = k.sb("sq", 128, CH, F32)
        sq2 = k.sb("sq2", 128, CH, BF16)
        mrow = k.sb("mrow", 1, CH, F32)
        rrow = k.sb("rrow", 1, CH, F32)
        sA = k.sb("sA", 1, CH, F32)
        sB = k.sb("sB", 1, CH, F32)
        ssr = k.sb("ssr", 16, CH, F32)
        ssb = k.sb("ssb", 16, CH, BF16)
        t1b = k.sb("t1b", 128, CH, BF16)
        t2b = k.sb("t2b", 128, CH, BF16)
        pbf = k.sb("pbf", 128, CH, BF16)
        tf32 = k.sb("tf32", 128, CH, F32)
        dd = k.sb("dd", 1, CH, F32)
        rd = k.sb("rd", 1, CH, F32)
        rdb = k.sb("rdb", 1, CH, BF16)
        obf = k.sb("obf", 64, CH, F32)
        yc = k.sb("yc", 128, CH, F32)

        psA = k.ps("psA", 1, CH)
        psB = k.ps("psB", 1, CH)
        psM = k.ps("psM", 128, CH)
        psR = k.ps("psR", 128, CH)
        psO = k.ps("psO", 128, CH)
        psSS = k.ps("psSS", 16, CH)
        psPM = k.ps("psPM", 128, CH)

        k.mm(psM[0:128, 0:16], ones_row[0:1, 0:128], qga[0:1, 0:16])
        k.cp(biasc[0:128, 0:16], psM[0:128, 0:16])

        def rms_rope(tiles, nt, inds_d2h, inds_h2d, nh, gain_col, outD, c):
            cs = slice(c * CH, (c + 1) * CH)
            for dt in range(nt):
                k.tt(sq2[0:128, 0:CH], tiles[dt][0:128, 0:CH],
                     tiles[dt][0:128, 0:CH], ALU.mult)
                k.mm(psSS[0:nh, 0:CH], inds_d2h[dt][0:128, 0:nh],
                     sq2[0:128, 0:CH], start=(dt == 0), stop=(dt == nt - 1))
            k.tsc(ssr[0:nh, 0:CH], psSS[0:nh, 0:CH], 1.0 / 64, EPS, ALU.mult, ALU.add)
            k.act(ssr[0:nh, 0:CH], ssr[0:nh, 0:CH], AF.Sqrt)
            k.op("vector", "reciprocal", ssr[0:nh, 0:CH], ssr[0:nh, 0:CH])
            if gain_col is not None:
                k.tsc(ssr[0:nh, 0:CH], ssr[0:nh, 0:CH], gain_col, None, ALU.mult)
            k.cp(ssb[0:nh, 0:CH], ssr[0:nh, 0:CH])
            for dt in range(nt):
                k.mm(psM[0:128, 0:CH], inds_h2d[dt][0:nh, 0:128], ssb[0:nh, 0:CH])
                k.mm(psPM[0:128, 0:CH], perm_b[0:128, 0:128], tiles[dt][0:128, 0:CH])
                k.tt(t1b[0:128, 0:CH], tiles[dt][0:128, 0:CH], Ct[0:128, cs], ALU.mult)
                k.tt(t2b[0:128, 0:CH], psPM[0:128, 0:CH], St[0:128, cs], ALU.mult)
                k.tt(t1b[0:128, 0:CH], t1b[0:128, 0:CH], t2b[0:128, 0:CH], ALU.add)
                k.tt(outD[dt], t1b[0:128, 0:CH], psM[0:128, 0:CH], ALU.mult)

        for c in range(NCH):
            cs = slice(c * CH, (c + 1) * CH)
            for dt in range(NDT):
                k.dma(xc[dt][0:128, 0:CH], xT[dt * 128:dt * 128 + 128, cs])
            emit_ln(k, xc, xn, sq, psA, psB, mrow, rrow, sA, sB, ones_col, psM, psR)
            for ot in range(8):
                for dt in range(NDT):
                    k.mm(psO[0:128, 0:CH], wq_b[dt][0:128, ot * 128:ot * 128 + 128],
                         xn[dt][0:128, 0:CH], start=(dt == 0), stop=(dt == NDT - 1))
                k.cp(qt[ot][0:128, 0:CH], psO[0:128, 0:CH])
            rms_rope(qt, 8, indt_b, ind_b, 16, qg8[0:16, 0:1],
                     [qt[i][0:128, 0:CH] for i in range(8)], c)
            for ot in range(8):
                k.dma(qD[ot * 128:ot * 128 + 128, cs], qt[ot][0:128, 0:CH])
            for ot in range(2):
                for dt in range(NDT):
                    k.mm(psO[0:128, 0:CH], wk_b[dt][0:128, ot * 128:ot * 128 + 128],
                         xn[dt][0:128, 0:CH], start=(dt == 0), stop=(dt == NDT - 1))
                k.cp(qt[ot][0:128, 0:CH], psO[0:128, 0:CH])
            rms_rope(qt, 2, kvd2h, kvh2d, 4, None,
                     [Kt[0][0:128, cs], Kt[1][0:128, cs]], c)
            for ss in range(4):
                for dt in range(NDT):
                    k.mm(psO[0:128, 0:KV], xn[dt][0:128, ss * 128:ss * 128 + 128],
                         wv_b[dt][0:128, 0:KV], start=(dt == 0), stop=(dt == NDT - 1))
                k.cp(Vt[c * 4 + ss][0:128, 0:KV], psO[0:128, 0:KV])

        psS2, psO2, psD2, psRB = psM, psO, psA, psR
        for c in range(NCH):
            cs = slice(c * CH, (c + 1) * CH)
            ktmax = 4 * c + 3
            for h in range(16):
                hp = (h % 2) * 64
                hd = h // 2
                g = h // 4
                gt = g // 2
                gp = (g % 2) * 64
                k.dma(qh[gp:gp + 64, 0:CH], qD[h * 64:h * 64 + 64, cs])
                for kt in range(ktmax + 1):
                    k.mm(psS2[0:128, 0:CH],
                         Kt[gt][gp:gp + 64, kt * 128:kt * 128 + 128],
                         qh[gp:gp + 64, 0:CH], start=True, stop=True)
                    if kt >= 4 * c:
                        j = kt - 4 * c
                        k.tt(tf32[0:128, 0:CH], psS2[0:128, 0:CH],
                             mneg_b[j][0:128, 0:CH], ALU.add)
                        k.act(pbf[0:128, 0:CH], tf32[0:128, 0:CH], AF.Exp,
                              bias=biasc[0:128, h:h + 1])
                    else:
                        k.act(pbf[0:128, 0:CH], psS2[0:128, 0:CH], AF.Exp,
                              bias=biasc[0:128, h:h + 1])
                    k.mm(psO2[0:64, 0:CH], Vt[kt][0:128, g * 64:g * 64 + 64],
                         pbf[0:128, 0:CH], start=(kt == 0), stop=(kt == ktmax))
                    k.mm(psD2[0:1, 0:CH], onesb_col[0:128, 0:1],
                         pbf[0:128, 0:CH], start=(kt == 0), stop=(kt == ktmax))
                k.cp(dd[0:1, 0:CH], psD2[0:1, 0:CH])
                k.op("vector", "reciprocal", rd[0:1, 0:CH], dd[0:1, 0:CH])
                k.cp(rdb[0:1, 0:CH], rd[0:1, 0:CH])
                k.mm(psRB[0:64, 0:CH], onesb_row[0:1, 0:64], rdb[0:1, 0:CH],
                     start=True, stop=True)
                k.cp(obf[0:64, 0:CH], psO2[0:64, 0:CH])
                k.tt(oT[hd][hp:hp + 64, 0:CH], obf[0:64, 0:CH],
                     psRB[0:64, 0:CH], ALU.mult)
            for dt in range(NDT):
                for kt in range(8):
                    k.mm(psPM[0:128, 0:CH], wp_b[kt][0:128, dt * 128:dt * 128 + 128],
                         oT[kt][0:128, 0:CH], start=(kt == 0), stop=(kt == 7))
                k.dma(xc[0][0:128, 0:CH], xT[dt * 128:dt * 128 + 128, cs])
                k.tt(yc[0:128, 0:CH], psPM[0:128, 0:CH], xc[0][0:128, 0:CH], ALU.add)
                k.dma(yT[dt * 128:dt * 128 + 128, cs], yc[0:128, 0:CH])
        k.build()
    return k


def _run(k, per_core_inputs, core_ids=None):
    in_maps = []
    for c in range(NCORES):
        m = dict(k.inputs)
        m.update(per_core_inputs[c])
        in_maps.append(m)
    res = run_bass_kernel_spmd(k.nc, in_maps, core_ids=list(range(NCORES)))
    return res.results


def run_CD(inp, xTs):
    wC = {
        "WqT": np.ascontiguousarray(inp["L0_attn_b_Wq_w"].T),
        "WkT": np.ascontiguousarray(inp["L0_attn_b_Wk_w"].T),
        "WvT": np.ascontiguousarray(inp["L0_attn_b_Wv_w"].T),
        "WwT": np.ascontiguousarray(inp["L0_attn_b_Ww_w"].T),
        "WrT": np.ascontiguousarray(inp["L0_attn_b_Wr_w"].T),
        "bq": inp["L0_attn_b_Wq_b"].reshape(-1, 1),
        "bk": inp["L0_attn_b_Wk_b"].reshape(-1, 1),
        "bv": inp["L0_attn_b_Wv_b"].reshape(-1, 1),
        "bw": inp["L0_attn_b_Ww_b"].reshape(-1, 1),
        "br": inp["L0_attn_b_Wr_b"].reshape(-1, 1),
    }
    kC = build_launch_C(wC)
    res = _run(kC, [{"xT": xTs[b]} for b in range(8)])
    qs = [res[b]["qT"] for b in range(8)]
    ks = [res[b]["kT"] for b in range(8)]
    us = [res[b]["uT"] for b in range(8)]
    rs = [res[b]["rT"] for b in range(8)]

    wDd = {
        "WoT": np.ascontiguousarray(inp["L0_attn_b_Wo_w"].T),
        "bo": inp["L0_attn_b_Wo_b"].reshape(-1, 1),
    }
    kD = build_launch_D(wDd)
    percore = []
    for c in range(8):
        cs = slice(c * CH, (c + 1) * CH)
        percore.append({
            "qA": np.concatenate([qs[b][:, cs] for b in range(8)], axis=0),
            "kA": np.concatenate([ks[b][:, cs] for b in range(8)], axis=0),
            "uA": np.concatenate([us[b][:, cs] for b in range(8)], axis=0),
            "rA": np.concatenate([rs[b][:, cs] for b in range(8)], axis=0),
            "xA": np.ascontiguousarray(
                np.concatenate([xTs[b][:, cs] for b in range(8)], axis=0)),
        })
    res = _run(kD, percore)
    out = []
    for b in range(8):
        out.append(np.concatenate(
            [res[c]["yA"][b * D:(b + 1) * D, :] for c in range(8)], axis=1))
    return out


def run_G(inp, xTs, w1b_cores, w2b_cores):
    wG = {
        "b1T": np.ascontiguousarray(inp["L1_ffn_b1"].T),
        "b2": inp["L1_ffn_b2"],
    }
    kG = build_launch_G(wG)
    w1ball = np.concatenate(w1b_cores, axis=0)
    w2ball = np.concatenate(w2b_cores, axis=0)
    res = _run(kG, [{"xT": xTs[b], "w1ball": w1ball, "w2ball": w2ball}
                    for b in range(8)])
    return [res[b]["yT"] for b in range(8)]


def kernel(**inp):
    x = np.ascontiguousarray(inp["L0_attn_a_x"])
    xTs = [np.ascontiguousarray(x[b].T) for b in range(8)]

    wA = {
        "W_in": inp["L0_attn_a_W_in"],
        "b_in": inp["L0_attn_a_b_in"].reshape(-1, 1),
        "W_g": inp["L0_attn_a_W_g"],
        "b_g": inp["L0_attn_a_b_g"].reshape(-1, 1),
        "W_out": inp["L0_attn_a_W_out"],
        "b_out": inp["L0_attn_a_b_out"].reshape(-1, 1),
    }
    kA = build_launch_A(wA)
    res = _run(kA, [{"xT": xTs[b]} for b in range(8)])
    xTs = [res[b]["yT"] for b in range(8)]

    wB = {
        "wgT": np.ascontiguousarray(inp["L0_ffn_w_gate"].T),
        "wuT": np.ascontiguousarray(inp["L0_ffn_w_up"].T),
        "wdT": np.ascontiguousarray(inp["L0_ffn_w_down"].T),
    }
    kB = build_launch_B(wB)
    res = _run(kB, [{"xT": xTs[b]} for b in range(8)])
    xTs = [res[b]["yT"] for b in range(8)]

    xTs = run_CD(inp, xTs)

    res = _run(kB, [{"xT": xTs[b]} for b in range(8)])
    xTs = [res[b]["yT"] for b in range(8)]

    wF = {
        "WqT": np.ascontiguousarray(inp["L1_attn_a_Wq"].T),
        "WkT": np.ascontiguousarray(inp["L1_attn_a_Wk"].T),
        "WvT": np.ascontiguousarray(inp["L1_attn_a_Wv"].T),
        "bq": inp["L1_attn_a_bq"].reshape(-1, 1),
        "bk": inp["L1_attn_a_bk"].reshape(-1, 1),
        "bv_row": inp["L1_attn_a_bv"].reshape(1, -1),
    }
    kF = build_launch_F(wF)
    res = _run(kF, [{"xT": xTs[b]} for b in range(8)])
    xTs = [res[b]["yT"] for b in range(8)]

    kW = build_launch_W()
    W1 = inp["L1_ffn_W1"]
    W2 = inp["L1_ffn_W2"]
    percore = []
    for e in range(8):
        w2re = np.ascontiguousarray(
            W2[e].reshape(4096, 8, 128).transpose(1, 0, 2).reshape(8 * 4096, 128))
        percore.append({"w1e": np.ascontiguousarray(W1[e]), "w2re": w2re})
    resW = _run(kW, percore)
    w1b_cores = [resW[e]["w1b"] for e in range(8)]
    w2b_cores = [resW[e]["w2b"] for e in range(8)]

    xTs = run_G(inp, xTs, w1b_cores, w2b_cores)

    wH = {
        "WqT": np.ascontiguousarray(inp["L1_attn_b_Wq"].T),
        "WkT": np.ascontiguousarray(inp["L1_attn_b_Wk"].T),
        "WvT": np.ascontiguousarray(inp["L1_attn_b_Wv"].T),
        "WprojT": np.ascontiguousarray(inp["L1_attn_b_Wproj"].T),
        "qg": inp["L1_attn_b_q_gain"].reshape(-1, 1),
        "qg_row": inp["L1_attn_b_q_gain"].reshape(1, -1),
    }
    kH = build_launch_H(wH)
    res = _run(kH, [{"xT": xTs[b]} for b in range(8)])
    xTs = [res[b]["yT"] for b in range(8)]

    xTs = run_G(inp, xTs, w1b_cores, w2b_cores)

    out = np.stack([np.ascontiguousarray(np.asarray(t).T) for t in xTs], axis=0)
    return out.astype(np.float32)


# version 02 (final)
# speedup vs baseline: 1.7009x; 1.7009x over previous
import numpy as np
import ml_dtypes
from contextlib import ExitStack

import concourse.bass as bass
import concourse.mybir as mybir
from concourse.bass_utils import run_bass_kernel_spmd

F32 = mybir.dt.float32
BF16 = mybir.dt.bfloat16
I32 = mybir.dt.int32
AF = mybir.ActivationFunctionType
ALU = mybir.AluOpType
AX = mybir.AxisListType

D = 1024
S = 4096
NCORES = 8
CH = 512
NCH = S // CH
NDT = D // 128

bf = ml_dtypes.bfloat16


class K:
    SEM_SPAN = 24000

    def __init__(self):
        self.nc = bass.Bass("TRN2", target_bir_lowering=False)
        self.ctx = ExitStack()
        self.script = []
        self.inputs = {}

    def dram(self, name, shape, dt, kind):
        return self.nc.dram_tensor(name, shape, dt, kind=kind).ap()

    def din(self, name, arr):
        a = np.ascontiguousarray(arr)
        t = self.nc.dram_tensor(name, list(a.shape), np_to_dt(a.dtype), kind="ExternalInput").ap()
        self.inputs[name] = a
        return t

    def sb(self, name, p, f, dt):
        return self.ctx.enter_context(self.nc.sbuf_tensor(name, [p, f], dt))

    def ps(self, name, p, f, dt=F32):
        return self.ctx.enter_context(self.nc.psum_tensor(name, [p, f], dt))

    def op(self, eng, meth, *args, inc=1, **kwargs):
        self.script.append((eng, meth, args, kwargs, inc))

    def dma(self, out, in_, **kw):
        self.op("sync", "dma_start", out, in_, inc=16, **kw)

    def mm(self, out, lhsT, rhs, start=True, stop=True):
        self.op("tensor", "matmul", out, lhsT, rhs, start=start, stop=stop)

    def act(self, out, in_, func, **kw):
        self.op("scalar", "activation", out, in_, func, **kw)

    def tt(self, out, a, b, alu):
        self.op("vector", "tensor_tensor", out, a, b, alu)

    def tsc(self, out, in_, s1, s2, op0, op1=None):
        if op1 is None:
            self.op("vector", "tensor_scalar", out, in_, s1, None, op0)
        else:
            self.op("vector", "tensor_scalar", out, in_, s1, s2, op0, op1)

    def stt(self, out, in0, scalar, in1, op0, op1):
        self.op("vector", "scalar_tensor_tensor", out, in0, scalar, in1, op0, op1)

    def cp(self, out, in_):
        self.op("vector", "tensor_copy", out, in_)

    def build(self):
        script = self.script
        n = len(script)
        run_inc = [None] * n
        need_wait = [True] * n
        i = 0
        while i < n:
            eng = script[i][0]
            j = i
            while j + 1 < n and script[j + 1][0] == eng:
                j += 1
            if eng == "tensor":
                total = sum(script[t][4] for t in range(i, j + 1))
                for t in range(i, j + 1):
                    run_inc[t] = 0
                    need_wait[t] = (t == i)
                run_inc[j] = total
            elif eng == "sync":
                for t in range(i, j + 1):
                    run_inc[t] = script[t][4]
                    need_wait[t] = (t == i)
            else:
                for t in range(i, j + 1):
                    run_inc[t] = script[t][4]
                    need_wait[t] = True
            i = j + 1

        plan = []
        cnt = 0
        cur = 0
        for idx, (eng, meth, args, kwargs, _inc) in enumerate(script):
            inc = run_inc[idx]
            rotate = cnt + inc > K.SEM_SPAN
            wait = (cur, cnt) if (cnt > 0 and (need_wait[idx] or rotate)) else None
            if rotate:
                cur += 1
                cnt = 0
            plan.append((eng, meth, args, kwargs, inc, wait, (cur, cnt + inc)))
            cnt += inc
        self.final_sem = (cur, cnt)
        sems = [self.ctx.enter_context(self.nc.semaphore(f"gs{i}"))
                for i in range(cur + 1)]

        by_eng = {}
        for item in plan:
            by_eng.setdefault(item[0], []).append(item)

        with self.nc.Block() as block:
            def mk(engname):
                def f(eng):
                    for (e, meth, args, kwargs, inc, wait, after) in by_eng.get(engname, []):
                        if wait is not None:
                            ws, wv = wait
                            eng.wait_ge(sems[ws], wv)
                        inst = getattr(eng, meth)(*args, **kwargs)
                        inst.then_inc(sems[after[0]], inc)
                    if engname == "sync":
                        eng.wait_ge(sems[self.final_sem[0]], self.final_sem[1])
                return f
            if "sync" in by_eng or True:
                block.sync(mk("sync"))
            if "tensor" in by_eng:
                block.tensor(mk("tensor"))
            if "scalar" in by_eng:
                block.scalar(mk("scalar"))
            if "vector" in by_eng:
                block.vector(mk("vector"))
            if "gpsimd" in by_eng:
                block.gpsimd(mk("gpsimd"))
        return self.nc


def np_to_dt(d):
    d = np.dtype(d)
    if d == np.float32:
        return F32
    if d == np.int32:
        return I32
    if d == ml_dtypes.bfloat16:
        return BF16
    raise ValueError(d)


def c_ones_f32_col():
    return np.ones((128, 1), np.float32)

def c_ones_bf_col():
    return np.ones((128, 1), bf)

def c_ones_bf_row():
    return np.ones((1, 128), bf)

def c_ind_h2d(dt):
    m = np.zeros((16, 128), np.float32)
    for p in range(128):
        h = 2 * dt + (p // 64)
        m[h, p] = 1.0
    return m.astype(bf)

def c_ind_d2h(dt):
    return np.ascontiguousarray(c_ind_h2d(dt).T)


def emit_ln(k, xc, xn_out, sq, psA, psB, mrow, rrow, sA, sB, ones_col,
            psM, psR, xn_off=0, w=CH):
    for dt in range(NDT):
        k.tt(sq[0:128, 0:w], xc[dt][0:128, 0:w], xc[dt][0:128, 0:w], ALU.mult)
        k.mm(psA[0:1, 0:w], ones_col[0:128, 0:1], xc[dt][0:128, 0:w],
             start=(dt == 0), stop=(dt == NDT - 1))
        k.mm(psB[0:1, 0:w], ones_col[0:128, 0:1], sq[0:128, 0:w],
             start=(dt == 0), stop=(dt == NDT - 1))
    k.tsc(mrow[0:1, 0:w], psA[0:1, 0:w], 1.0 / D, None, ALU.mult)
    k.tsc(sB[0:1, 0:w], psB[0:1, 0:w], 1.0 / D, None, ALU.mult)
    k.tt(sA[0:1, 0:w], mrow[0:1, 0:w], mrow[0:1, 0:w], ALU.mult)
    k.tt(sB[0:1, 0:w], sB[0:1, 0:w], sA[0:1, 0:w], ALU.subtract)
    k.tsc(sB[0:1, 0:w], sB[0:1, 0:w], 1e-5, None, ALU.add)
    k.act(sA[0:1, 0:w], sB[0:1, 0:w], AF.Sqrt)
    k.op("vector", "reciprocal", rrow[0:1, 0:w], sA[0:1, 0:w])
    k.mm(psM[0:128, 0:w], ones_f32_row_g[0:1, 0:128], mrow[0:1, 0:w])
    k.mm(psR[0:128, 0:w], ones_f32_row_g[0:1, 0:128], rrow[0:1, 0:w])
    for dt in range(NDT):
        k.tt(sq[0:128, 0:w], xc[dt][0:128, 0:w], psM[0:128, 0:w], ALU.subtract)
        k.tt(xn_out[dt][0:128, xn_off:xn_off + w], sq[0:128, 0:w],
             psR[0:128, 0:w], ALU.mult)


ones_f32_row_g = None


def emit_wconv(k, wap, tiles, scratch, scale=None):
    R = wap.shape[0]
    C = wap.shape[1]
    r0 = 0
    i = 0
    while r0 < R:
        r = min(128, R - r0)
        k.dma(scratch[0:r, 0:C], wap[r0:r0 + r, 0:C])
        if scale is None:
            k.cp(tiles[i][0:r, 0:C], scratch[0:r, 0:C])
        else:
            k.tsc(tiles[i][0:r, 0:C], scratch[0:r, 0:C], scale, None, ALU.mult)
        r0 += r
        i += 1


def build_launch_A(weights):
    global ones_f32_row_g
    k = K()
    nc = k.nc
    xT = k.dram("xT", [D, S], F32, "ExternalInput")
    yT = k.dram("yT", [D, S], F32, "ExternalOutput")
    w_in = k.din("w_in", weights["W_in"])
    b_in = k.din("b_in", weights["b_in"])
    w_g = k.din("w_g", weights["W_g"])
    b_g = k.din("b_g", weights["b_g"])
    w_out = k.din("w_out", weights["W_out"])
    b_out = k.din("b_out", weights["b_out"])
    onc = k.din("onc", c_ones_f32_col())
    onr = k.din("onr", np.ones((1, 128), np.float32))

    E2 = 1032
    combD = k.dram("combD", [E2, S], BF16, "Internal")
    with k.ctx:
        ones_col = k.sb("ones_col", 128, 1, F32)
        ones_row = k.sb("ones_row", 1, 128, F32)
        ones_f32_row_g = ones_row
        k.dma(ones_col[0:128, 0:1], onc[0:128, 0:1])
        k.dma(ones_row[0:1, 0:128], onr[0:1, 0:128])

        wsc = k.sb("wsc", 128, 1032, F32)
        win_b = [k.sb(f"win{i}", 128, E2, BF16) for i in range(8)]
        emit_wconv(k, w_in, win_b, wsc)
        wg_b = [k.sb(f"wg{i}", 128, 8, BF16) for i in range(9)]
        emit_wconv(k, w_g, wg_b, wsc)
        wout_b = [k.sb(f"wout{i}", 128, D, BF16) for i in range(9)]
        emit_wconv(k, w_out, wout_b, wsc)
        bin_s = [k.sb(f"bin{i}", 128, 1, F32) for i in range(9)]
        for i in range(9):
            r = 128 if i < 8 else 8
            k.dma(bin_s[i][0:r, 0:1], b_in[i * 128:i * 128 + r, 0:1])
        bg_s = k.sb("bg", 8, 1, F32)
        k.dma(bg_s[0:8, 0:1], b_g[0:8, 0:1])
        bout_s = [k.sb(f"bout{i}", 128, 1, F32) for i in range(8)]
        for i in range(8):
            k.dma(bout_s[i][0:128, 0:1], b_out[i * 128:i * 128 + 128, 0:1])

        xc = [k.sb(f"xc{i}", 128, CH, F32) for i in range(NDT)]
        xn = [k.sb(f"xn{i}", 128, CH, BF16) for i in range(NDT)]
        comb = [k.sb(f"comb{i}", 128 if i < 8 else 8, CH, BF16) for i in range(9)]
        gS = k.sb("gS", 8, S, F32)
        siS = k.sb("siS", 8, S, F32)
        stS = k.sb("stS", 8, S, F32)
        st_bf = k.sb("st_bf", 8, S, BF16)
        sq = k.sb("sq", 128, CH, F32)
        mrow = k.sb("mrow", 1, CH, F32)
        rrow = k.sb("rrow", 1, CH, F32)
        sA = k.sb("sA", 1, CH, F32)
        sB = k.sb("sB", 1, CH, F32)
        yc = k.sb("yc", 128, CH, F32)

        psA = k.ps("psA", 1, CH)
        psB = k.ps("psB", 1, CH)
        psM = k.ps("psM", 128, CH)
        psR = k.ps("psR", 128, CH)
        psO = k.ps("psO", 128, CH)
        psG = k.ps("psG", 8, CH)

        for c in range(NCH):
            for dt in range(NDT):
                k.dma(xc[dt][0:128, 0:CH], xT[dt * 128:dt * 128 + 128, c * CH:(c + 1) * CH])
            emit_ln(k, xc, xn, sq, psA, psB, mrow, rrow, sA, sB, ones_col,
                    psM, psR)
            for ot in range(9):
                rows = 128 if ot < 8 else 8
                for dt in range(NDT):
                    k.mm(psO[0:rows, 0:CH],
                         win_b[dt][0:128, ot * 128:ot * 128 + rows],
                         xn[dt][0:128, 0:CH],
                         start=(dt == 0), stop=(dt == NDT - 1))
                k.tsc(comb[ot][0:rows, 0:CH], psO[0:rows, 0:CH],
                      bin_s[ot][0:rows, 0:1], None, ALU.add)
                k.dma(combD[ot * 128:ot * 128 + rows, c * CH:(c + 1) * CH],
                      comb[ot][0:rows, 0:CH])
            for kt in range(9):
                rows = 128 if kt < 8 else 8
                k.mm(psG[0:8, 0:CH],
                     wg_b[kt][0:rows, 0:8],
                     comb[kt][0:rows, 0:CH],
                     start=(kt == 0), stop=(kt == 8))
            k.act(gS[0:8, c * CH:(c + 1) * CH], psG[0:8, 0:CH], AF.Sigmoid,
                  bias=bg_s[0:8, 0:1])
            k.cp(siS[0:8, c * CH:(c + 1) * CH], comb[8][0:8, 0:CH])

        k.tsc(stS[0:8, 0:S], gS[0:8, 0:S], -1.0, 1.0, ALU.mult, ALU.add)
        k.tt(siS[0:8, 0:S], stS[0:8, 0:S], siS[0:8, 0:S], ALU.mult)
        k.op("vector", "tensor_tensor_scan", stS[0:8, 0:S], gS[0:8, 0:S],
             siS[0:8, 0:S], 0.0, ALU.mult, ALU.add)
        k.cp(st_bf[0:8, 0:S], stS[0:8, 0:S])

        for c in range(NCH):
            for kt in range(8):
                k.dma(comb[kt][0:128, 0:CH],
                      combD[kt * 128:kt * 128 + 128, c * CH:(c + 1) * CH])
            for dt in range(NDT):
                for kt in range(9):
                    if kt < 8:
                        rhs = comb[kt][0:128, 0:CH]
                        lhs = wout_b[kt][0:128, dt * 128:dt * 128 + 128]
                    else:
                        rhs = st_bf[0:8, c * CH:(c + 1) * CH]
                        lhs = wout_b[8][0:8, dt * 128:dt * 128 + 128]
                    k.mm(psO[0:128, 0:CH], lhs, rhs,
                         start=(kt == 0), stop=(kt == 8))
                k.dma(xc[0][0:128, 0:CH], xT[dt * 128:dt * 128 + 128, c * CH:(c + 1) * CH])
                k.stt(yc[0:128, 0:CH], psO[0:128, 0:CH], bout_s[dt][0:128, 0:1],
                      xc[0][0:128, 0:CH], ALU.add, ALU.add)
                k.dma(yT[dt * 128:dt * 128 + 128, c * CH:(c + 1) * CH], yc[0:128, 0:CH])

        k.build()
    return k


def build_launch_B(weights):
    global ones_f32_row_g
    k = K()
    H = 3072
    NHT = H // 128
    xT = k.dram("xT", [D, S], F32, "ExternalInput")
    yT = k.dram("yT", [D, S], F32, "ExternalOutput")
    hD = k.dram("hD", [H, S], BF16, "Internal")
    wgT = k.din("wgT", weights["wgT"])
    wuT = k.din("wuT", weights["wuT"])
    wdT = k.din("wdT", weights["wdT"])
    onc = k.din("onc", c_ones_f32_col())
    onr = k.din("onr", np.ones((1, 128), np.float32))
    SC = 1.0 / 127.0

    with k.ctx:
        ones_col = k.sb("ones_col", 128, 1, F32)
        ones_row = k.sb("ones_row", 1, 128, F32)
        ones_f32_row_g = ones_row
        k.dma(ones_col[0:128, 0:1], onc[0:128, 0:1])
        k.dma(ones_row[0:1, 0:128], onr[0:1, 0:128])

        wsc = k.sb("wsc", 128, H, I32)
        pool = [k.sb(f"wp{i}", 128, H, BF16) for i in range(16)]
        wg_b = pool[0:8]
        wu_b = pool[8:16]
        def wd_ap(j, c0, c1):
            return pool[j // 3][0:128, (j % 3) * D + c0:(j % 3) * D + c1]

        xc = [k.sb(f"xc{i}", 128, CH, F32) for i in range(NDT)]
        xn = [k.sb(f"xn{i}", 128, CH, BF16) for i in range(NDT)]
        hc = [k.sb(f"hc{i}", 128, CH, BF16) for i in range(NHT)]
        gt = k.sb("gt", 128, CH, BF16)
        sq = k.sb("sq", 128, CH, F32)
        mrow = k.sb("mrow", 1, CH, F32)
        rrow = k.sb("rrow", 1, CH, F32)
        sA = k.sb("sA", 1, CH, F32)
        sB = k.sb("sB", 1, CH, F32)
        yc = k.sb("yc", 128, CH, F32)

        psA = k.ps("psA", 1, CH)
        psB = k.ps("psB", 1, CH)
        psM = k.ps("psM", 128, CH)
        psR = k.ps("psR", 128, CH)
        psG = k.ps("psG", 128, CH)
        psU = k.ps("psU", 128, CH)
        psY = k.ps("psY", 128, CH)

        emit_wconv(k, wgT, wg_b, wsc, scale=SC)
        emit_wconv(k, wuT, wu_b, wsc, scale=SC)

        for c in range(NCH):
            for dt in range(NDT):
                k.dma(xc[dt][0:128, 0:CH], xT[dt * 128:dt * 128 + 128, c * CH:(c + 1) * CH])
            emit_ln(k, xc, xn, sq, psA, psB, mrow, rrow, sA, sB, ones_col, psM, psR)
            for ht in range(NHT):
                for dt in range(NDT):
                    k.mm(psG[0:128, 0:CH], wg_b[dt][0:128, ht * 128:ht * 128 + 128],
                         xn[dt][0:128, 0:CH], start=(dt == 0), stop=(dt == NDT - 1))
                for dt in range(NDT):
                    k.mm(psU[0:128, 0:CH], wu_b[dt][0:128, ht * 128:ht * 128 + 128],
                         xn[dt][0:128, 0:CH], start=(dt == 0), stop=(dt == NDT - 1))
                k.act(gt[0:128, 0:CH], psG[0:128, 0:CH], AF.Silu)
                k.tt(hc[ht][0:128, 0:CH], gt[0:128, 0:CH], psU[0:128, 0:CH], ALU.mult)
                k.dma(hD[ht * 128:ht * 128 + 128, c * CH:(c + 1) * CH], hc[ht][0:128, 0:CH])

        for j in range(NHT):
            k.dma(wsc[0:128, 0:D], wdT[j * 128:j * 128 + 128, 0:D])
            k.tsc(wd_ap(j, 0, D), wsc[0:128, 0:D], SC, None, ALU.mult)
        for c in range(NCH):
            for ht in range(NHT):
                k.dma(hc[ht][0:128, 0:CH], hD[ht * 128:ht * 128 + 128, c * CH:(c + 1) * CH])
            for dt in range(NDT):
                for ht in range(NHT):
                    k.mm(psY[0:128, 0:CH], wd_ap(ht, dt * 128, dt * 128 + 128),
                         hc[ht][0:128, 0:CH], start=(ht == 0), stop=(ht == NHT - 1))
                k.dma(xc[0][0:128, 0:CH], xT[dt * 128:dt * 128 + 128, c * CH:(c + 1) * CH])
                k.stt(yc[0:128, 0:CH], psY[0:128, 0:CH], 0.0,
                      xc[0][0:128, 0:CH], ALU.add, ALU.add)
                k.dma(yT[dt * 128:dt * 128 + 128, c * CH:(c + 1) * CH], yc[0:128, 0:CH])
        k.build()
    return k


def build_launch_C(weights):
    global ones_f32_row_g
    k = K()
    xT = k.dram("xT", [D, S], F32, "ExternalInput")
    qT = k.dram("qT", [D, S], BF16, "ExternalOutput")
    kT_ = k.dram("kT", [D, S], BF16, "ExternalOutput")
    uT = k.dram("uT", [D, S], BF16, "ExternalOutput")
    rT = k.dram("rT", [16, S], BF16, "ExternalOutput")
    wq = k.din("wq", weights["WqT"])
    wk = k.din("wk", weights["WkT"])
    wv = k.din("wv", weights["WvT"])
    ww = k.din("ww", weights["WwT"])
    wr = k.din("wr", weights["WrT"])
    bq = k.din("bq", weights["bq"])
    bk = k.din("bk", weights["bk"])
    bv = k.din("bv", weights["bv"])
    bw = k.din("bw", weights["bw"])
    br = k.din("br", weights["br"])
    indh = k.din("indh", np.concatenate([c_ind_h2d(dt) for dt in range(8)], axis=0))
    onc = k.din("onc", c_ones_f32_col())
    onr = k.din("onr", np.ones((1, 128), np.float32))

    with k.ctx:
        ones_col = k.sb("ones_col", 128, 1, F32)
        ones_row = k.sb("ones_row", 1, 128, F32)
        ones_f32_row_g = ones_row
        k.dma(ones_col[0:128, 0:1], onc[0:128, 0:1])
        k.dma(ones_row[0:1, 0:128], onr[0:1, 0:128])
        ind_b = [k.sb(f"ind{i}", 16, 128, BF16) for i in range(8)]
        for i in range(8):
            k.dma(ind_b[i][0:16, 0:128], indh[i * 16:i * 16 + 16, 0:128])

        wsc = k.sb("wsc", 128, D, F32)
        wq_b = [k.sb(f"wqb{i}", 128, D, BF16) for i in range(8)]
        wk_b = [k.sb(f"wkb{i}", 128, D, BF16) for i in range(8)]
        wv_b = [k.sb(f"wvb{i}", 128, D, BF16) for i in range(8)]
        ww_b = [k.sb(f"wwb{i}", 128, 16, BF16) for i in range(8)]
        wr_b = [k.sb(f"wrb{i}", 128, 16, BF16) for i in range(8)]
        emit_wconv(k, wq, wq_b, wsc)
        emit_wconv(k, wk, wk_b, wsc)
        emit_wconv(k, wv, wv_b, wsc)
        emit_wconv(k, ww, ww_b, wsc)
        emit_wconv(k, wr, wr_b, wsc)
        bq_s = [k.sb(f"bqs{i}", 128, 1, F32) for i in range(8)]
        bk_s = [k.sb(f"bks{i}", 128, 1, F32) for i in range(8)]
        bv_s = [k.sb(f"bvs{i}", 128, 1, F32) for i in range(8)]
        for i in range(8):
            k.dma(bq_s[i][0:128, 0:1], bq[i * 128:i * 128 + 128, 0:1])
            k.dma(bk_s[i][0:128, 0:1], bk[i * 128:i * 128 + 128, 0:1])
            k.dma(bv_s[i][0:128, 0:1], bv[i * 128:i * 128 + 128, 0:1])
        bw_s = k.sb("bws", 16, 1, F32)
        br_s = k.sb("brs", 16, 1, F32)
        k.dma(bw_s[0:16, 0:1], bw[0:16, 0:1])
        k.dma(br_s[0:16, 0:1], br[0:16, 0:1])

        xc = [k.sb(f"xc{i}", 128, CH, F32) for i in range(NDT)]
        xn = [k.sb(f"xn{i}", 128, CH, BF16) for i in range(NDT)]
        ob = k.sb("ob", 128, CH, BF16)
        vb = k.sb("vb", 128, CH, BF16)
        wrow = k.sb("wrow", 16, CH, BF16)
        sq = k.sb("sq", 128, CH, F32)
        mrow = k.sb("mrow", 1, CH, F32)
        rrow = k.sb("rrow", 1, CH, F32)
        sA = k.sb("sA", 1, CH, F32)
        sB = k.sb("sB", 1, CH, F32)

        psA = k.ps("psA", 1, CH)
        psB = k.ps("psB", 1, CH)
        psM = k.ps("psM", 128, CH)
        psR = k.ps("psR", 128, CH)
        psO = k.ps("psO", 128, CH)
        psW = k.ps("psW", 16, CH)

        for c in range(NCH):
            cs = slice(c * CH, (c + 1) * CH)
            for dt in range(NDT):
                k.dma(xc[dt][0:128, 0:CH], xT[dt * 128:dt * 128 + 128, cs])
            emit_ln(k, xc, xn, sq, psA, psB, mrow, rrow, sA, sB, ones_col, psM, psR)
            for (wb, bs, outd) in ((wq_b, bq_s, qT), (wk_b, bk_s, kT_)):
                for ot in range(8):
                    for dt in range(NDT):
                        k.mm(psO[0:128, 0:CH], wb[dt][0:128, ot * 128:ot * 128 + 128],
                             xn[dt][0:128, 0:CH], start=(dt == 0), stop=(dt == NDT - 1))
                    k.tsc(ob[0:128, 0:CH], psO[0:128, 0:CH], bs[ot][0:128, 0:1],
                          None, ALU.add)
                    k.dma(outd[ot * 128:ot * 128 + 128, cs], ob[0:128, 0:CH])
            for dt in range(NDT):
                k.mm(psW[0:16, 0:CH], ww_b[dt][0:128, 0:16], xn[dt][0:128, 0:CH],
                     start=(dt == 0), stop=(dt == NDT - 1))
            k.act(wrow[0:16, 0:CH], psW[0:16, 0:CH], AF.Sigmoid, bias=bw_s[0:16, 0:1])
            for dt in range(NDT):
                k.mm(psW[0:16, 0:CH], wr_b[dt][0:128, 0:16], xn[dt][0:128, 0:CH],
                     start=(dt == 0), stop=(dt == NDT - 1))
            k.act(ob[0:16, 0:CH], psW[0:16, 0:CH], AF.Sigmoid, bias=br_s[0:16, 0:1])
            k.dma(rT[0:16, cs], ob[0:16, 0:CH])
            for ot in range(8):
                for dt in range(NDT):
                    k.mm(psO[0:128, 0:CH], wv_b[dt][0:128, ot * 128:ot * 128 + 128],
                         xn[dt][0:128, 0:CH], start=(dt == 0), stop=(dt == NDT - 1))
                k.tsc(vb[0:128, 0:CH], psO[0:128, 0:CH], bv_s[ot][0:128, 0:1],
                      None, ALU.add)
                k.mm(psM[0:128, 0:CH], ind_b[ot][0:16, 0:128], wrow[0:16, 0:CH])
                k.tt(ob[0:128, 0:CH], vb[0:128, 0:CH], psM[0:128, 0:CH], ALU.mult)
                k.dma(uT[ot * 128:ot * 128 + 128, cs], ob[0:128, 0:CH])
        k.build()
    return k


def build_launch_D(weights):
    k = K()
    P = CH
    qA = k.dram("qA", [8 * D, P], BF16, "ExternalInput")
    kA = k.dram("kA", [8 * D, P], BF16, "ExternalInput")
    uA = k.dram("uA", [8 * D, P], BF16, "ExternalInput")
    rA = k.dram("rA", [128, P], BF16, "ExternalInput")
    xA = k.dram("xA", [8 * D, P], F32, "ExternalInput")
    yA = k.dram("yA", [8 * D, P], F32, "ExternalOutput")
    wo = k.din("wo", weights["WoT"])
    bo = k.din("bo", weights["bo"])
    indh = k.din("indh", np.concatenate([c_ind_h2d(dt) for dt in range(8)], axis=0))
    indt = k.din("indt", np.concatenate([c_ind_d2h(dt) for dt in range(8)], axis=1))

    with k.ctx:
        ind_b = [k.sb(f"ind{i}", 16, 128, BF16) for i in range(8)]
        indt_b = [k.sb(f"indt{i}", 128, 16, BF16) for i in range(8)]
        for i in range(8):
            k.dma(ind_b[i][0:16, 0:128], indh[i * 16:i * 16 + 16, 0:128])
            k.dma(indt_b[i][0:128, 0:16], indt[0:128, i * 16:i * 16 + 16])
        wsc = k.sb("wsc", 128, D, F32)
        wo_b = [k.sb(f"wob{i}", 128, D, BF16) for i in range(8)]
        emit_wconv(k, wo, wo_b, wsc)
        bo_s = [k.sb(f"bos{i}", 128, 1, F32) for i in range(8)]
        for i in range(8):
            k.dma(bo_s[i][0:128, 0:1], bo[i * 128:i * 128 + 128, 0:1])

        qb = [k.sb(f"qb{i}", 128, P, BF16) for i in range(8)]
        kb = [k.sb(f"kb{i}", 128, P, BF16) for i in range(8)]
        ub = [k.sb(f"ub{i}", 128, P, BF16) for i in range(8)]
        rb = k.sb("rb", 16, P, BF16)
        mb = [k.sb(f"mb{i}", 128, P, BF16) for i in range(8)]
        acc = [k.sb(f"acc{i}", 128, P, F32) for i in range(8)]
        gt = [k.sb(f"gt{i}", 128, P, BF16) for i in range(8)]
        scb = k.sb("scb", 16, P, BF16)
        tmp = k.sb("tmp", 128, P, BF16)
        xcd = k.sb("xcd", 128, P, F32)
        yc = k.sb("yc", 128, P, F32)

        psS = k.ps("psS", 16, P)
        psB_ = k.ps("psB_", 128, P)
        psY = k.ps("psY", 128, P)

        for b in range(8):
            for dt in range(8):
                k.dma(qb[dt][0:128, 0:P], qA[b * D + dt * 128:b * D + dt * 128 + 128, 0:P])
            k.dma(rb[0:16, 0:P], rA[b * 16:b * 16 + 16, 0:P])
            for sp in range(8):
                for dt in range(8):
                    k.dma(kb[dt][0:128, 0:P], kA[sp * D + dt * 128:sp * D + dt * 128 + 128, 0:P])
                    k.dma(ub[dt][0:128, 0:P], uA[sp * D + dt * 128:sp * D + dt * 128 + 128, 0:P])
                for dt in range(8):
                    k.tt(mb[dt][0:128, 0:P], qb[dt][0:128, 0:P], kb[dt][0:128, 0:P], ALU.mult)
                for dt in range(8):
                    k.mm(psS[0:16, 0:P], indt_b[dt][0:128, 0:16], mb[dt][0:128, 0:P],
                         start=(dt == 0), stop=(dt == 7))
                k.cp(scb[0:16, 0:P], psS[0:16, 0:P])
                for dt in range(8):
                    k.mm(psB_[0:128, 0:P], ind_b[dt][0:16, 0:128], scb[0:16, 0:P])
                    k.tt(tmp[0:128, 0:P], ub[dt][0:128, 0:P], psB_[0:128, 0:P], ALU.mult)
                    if sp == 0:
                        k.cp(acc[dt][0:128, 0:P], tmp[0:128, 0:P])
                    else:
                        k.tt(acc[dt][0:128, 0:P], acc[dt][0:128, 0:P], tmp[0:128, 0:P], ALU.add)
            for dt in range(8):
                k.mm(psB_[0:128, 0:P], ind_b[dt][0:16, 0:128], rb[0:16, 0:P])
                k.tt(gt[dt][0:128, 0:P], acc[dt][0:128, 0:P], psB_[0:128, 0:P], ALU.mult)
            for dt in range(8):
                for kt in range(8):
                    k.mm(psY[0:128, 0:P], wo_b[kt][0:128, dt * 128:dt * 128 + 128],
                         gt[kt][0:128, 0:P], start=(kt == 0), stop=(kt == 7))
                k.dma(xcd[0:128, 0:P], xA[b * D + dt * 128:b * D + dt * 128 + 128, 0:P])
                k.stt(yc[0:128, 0:P], psY[0:128, 0:P], bo_s[dt][0:128, 0:1],
                      xcd[0:128, 0:P], ALU.add, ALU.add)
                k.dma(yA[b * D + dt * 128:b * D + dt * 128 + 128, 0:P], yc[0:128, 0:P])
        k.build()
    return k


def c_masks_F():
    mneg = np.zeros((6, 128, CH), np.float32)
    mbin = np.zeros((6, 128, CH), np.float32)
    for j in range(6):
        kr = np.arange(128)[:, None]
        qr = np.arange(CH)[None, :]
        delta = 128 * (j - 1) + kr - qr
        inw = (delta >= -128) & (delta <= 127)
        mneg[j] = np.where(inw, 0.0, -30000.0)
        mbin[j] = inw.astype(np.float32)
    return mneg, mbin


def c_wcm_F():
    i = np.arange(S)
    lo = np.maximum(0, i - 128)
    hi = np.minimum(S - 1, i + 127)
    cnt = hi - lo + 1
    return (S - cnt).astype(np.float32).reshape(1, S)


def build_launch_F(weights):
    global ones_f32_row_g
    k = K()
    xT = k.dram("xT", [D, S], F32, "ExternalInput")
    yT = k.dram("yT", [D, S], F32, "ExternalOutput")
    qD = k.dram("qD", [D, S], BF16, "Internal")
    kD = k.dram("kD", [D, S], BF16, "Internal")
    vD = k.dram("vD", [S, D], BF16, "Internal")
    wq = k.din("wq", weights["WqT"])
    wk = k.din("wk", weights["WkT"])
    wv = k.din("wv", weights["WvT"])
    bq = k.din("bq", weights["bq"])
    bk = k.din("bk", weights["bk"])
    bvr = k.din("bvr", weights["bv_row"])
    mnegA, mbinA = c_masks_F()
    mnegI = k.din("mnegI", mnegA.reshape(6 * 128, CH))
    mbinI = k.din("mbinI", mbinA.reshape(6 * 128, CH).astype(bf))
    wcmI = k.din("wcmI", c_wcm_F())
    onc = k.din("onc", c_ones_f32_col())
    onr = k.din("onr", np.ones((1, 128), np.float32))
    oncb = k.din("oncb", np.ones((128, 1), bf))
    onrb = k.din("onrb", np.ones((1, 512), bf))

    with k.ctx:
        ones_col = k.sb("ones_col", 128, 1, F32)
        ones_row = k.sb("ones_row", 1, 128, F32)
        ones_f32_row_g = ones_row
        onesb_col = k.sb("onesb_col", 128, 1, BF16)
        onesb_row = k.sb("onesb_row", 1, 512, BF16)
        k.dma(ones_col[0:128, 0:1], onc[0:128, 0:1])
        k.dma(ones_row[0:1, 0:128], onr[0:1, 0:128])
        k.dma(onesb_col[0:128, 0:1], oncb[0:128, 0:1])
        k.dma(onesb_row[0:1, 0:512], onrb[0:1, 0:512])
        mneg_b = [k.sb(f"mneg{j}", 128, CH, F32) for j in range(6)]
        mbin_b = [k.sb(f"mbin{j}", 128, CH, BF16) for j in range(6)]
        for j in range(6):
            k.dma(mneg_b[j][0:128, 0:CH], mnegI[j * 128:j * 128 + 128, 0:CH])
            k.dma(mbin_b[j][0:128, 0:CH], mbinI[j * 128:j * 128 + 128, 0:CH])
        wcm = k.sb("wcm", 1, S, F32)
        k.dma(wcm[0:1, 0:S], wcmI[0:1, 0:S])

        wsc = k.sb("wsc", 128, D, F32)
        wq_b = [k.sb(f"wqb{i}", 128, D, BF16) for i in range(8)]
        wk_b = [k.sb(f"wkb{i}", 128, D, BF16) for i in range(8)]
        wv_b = [k.sb(f"wvb{i}", 128, D, BF16) for i in range(8)]
        emit_wconv(k, wq, wq_b, wsc)
        emit_wconv(k, wk, wk_b, wsc)
        emit_wconv(k, wv, wv_b, wsc)
        for i in range(8):
            k.tsc(wq_b[i][0:128, 0:D], wq_b[i][0:128, 0:D], 0.125, None, ALU.mult)
        bq_s = [k.sb(f"bqs{i}", 128, 1, F32) for i in range(8)]
        bk_s = [k.sb(f"bks{i}", 128, 1, F32) for i in range(8)]
        for i in range(8):
            k.dma(bq_s[i][0:128, 0:1], bq[i * 128:i * 128 + 128, 0:1])
            k.tsc(bq_s[i][0:128, 0:1], bq_s[i][0:128, 0:1], 0.125, None, ALU.mult)
            k.dma(bk_s[i][0:128, 0:1], bk[i * 128:i * 128 + 128, 0:1])
        bv_row = k.sb("bv_row", 1, D, F32)
        k.dma(bv_row[0:1, 0:D], bvr[0:1, 0:D])
        bv_rb = k.sb("bv_rb", 1, D, BF16)
        k.cp(bv_rb[0:1, 0:D], bv_row[0:1, 0:D])

        xc = [k.sb(f"xc{i}", 128, CH, F32) for i in range(NDT)]
        xn = [k.sb(f"xn{i}", 128, CH, BF16) for i in range(NDT)]
        ob = k.sb("ob", 128, CH, BF16)
        sq = k.sb("sq", 128, CH, F32)
        mrow = k.sb("mrow", 1, CH, F32)
        rrow = k.sb("rrow", 1, CH, F32)
        sA = k.sb("sA", 1, CH, F32)
        sB = k.sb("sB", 1, CH, F32)
        vt_row = k.sb("vt_row", 1, D, F32)
        vt_rb = k.sb("vt_rb", 1, D, BF16)

        psA = k.ps("psA", 1, CH)
        psB = k.ps("psB", 1, CH)
        psM = k.ps("psM", 128, CH)
        psR = k.ps("psR", 128, CH)
        psO = k.ps("psO", 128, CH)
        psVT = k.ps("psVT", 1, D)

        for c in range(NCH):
            cs = slice(c * CH, (c + 1) * CH)
            for dt in range(NDT):
                k.dma(xc[dt][0:128, 0:CH], xT[dt * 128:dt * 128 + 128, cs])
            emit_ln(k, xc, xn, sq, psA, psB, mrow, rrow, sA, sB, ones_col, psM, psR)
            for (wb, bs, outd) in ((wq_b, bq_s, qD), (wk_b, bk_s, kD)):
                for ot in range(8):
                    for dt in range(NDT):
                        k.mm(psO[0:128, 0:CH], wb[dt][0:128, ot * 128:ot * 128 + 128],
                             xn[dt][0:128, 0:CH], start=(dt == 0), stop=(dt == NDT - 1))
                    k.tsc(ob[0:128, 0:CH], psO[0:128, 0:CH], bs[ot][0:128, 0:1],
                          None, ALU.add)
                    k.dma(outd[ot * 128:ot * 128 + 128, cs], ob[0:128, 0:CH])
            for ss in range(4):
                for oh in range(2):
                    for dt in range(NDT):
                        k.mm(psO[0:128, 0:CH], xn[dt][0:128, ss * 128:ss * 128 + 128],
                             wv_b[dt][0:128, oh * 512:oh * 512 + 512],
                             start=(dt == 0), stop=False)
                    k.mm(psO[0:128, 0:CH],
                         onesb_row[0:1, 0:128],
                         bv_rb[0:1, oh * 512:oh * 512 + 512], start=False, stop=True)
                    k.cp(ob[0:128, 0:CH], psO[0:128, 0:CH])
                    k.dma(vD[c * CH + ss * 128:c * CH + ss * 128 + 128,
                            oh * 512:oh * 512 + 512], ob[0:128, 0:CH])

        qb = [k.sb(f"qb{i}", 128, CH, BF16) for i in range(8)]
        kw = [k.sb(f"kw{i}", 128, 6 * 128, BF16) for i in range(8)]
        vw = [k.sb(f"vw{i}", 128, D, BF16) for i in range(6)]
        oT = [k.sb(f"oT{i}", 128, CH, BF16) for i in range(8)]
        fbf = k.sb("fbf", 128, CH, BF16)
        tf32 = k.sb("tf32", 128, CH, F32)
        dd = k.sb("dd", 1, CH, F32)
        rd = k.sb("rd", 1, CH, F32)
        rdb = k.sb("rdb", 1, CH, BF16)
        obf = k.sb("obf", 64, CH, F32)
        psS2, psO2, psD2, psRB = psM, psO, psA, psR

        for kt in range(32):
            k.dma(vw[0][0:128, 0:D], vD[kt * 128:kt * 128 + 128, 0:D])
            for oh in range(2):
                k.mm(psVT[0:1, oh * 512:oh * 512 + 512],
                     onesb_col[0:128, 0:1], vw[0][0:128, oh * 512:oh * 512 + 512],
                     start=(kt == 0 and oh == 0), stop=(kt == 31 and oh == 1))
        k.cp(vt_row[0:1, 0:D], psVT[0:1, 0:D])
        k.cp(vt_rb[0:1, 0:D], vt_row[0:1, 0:D])

        for c in range(NCH):
            cs = slice(c * CH, (c + 1) * CH)
            kt0 = max(0, 4 * c - 1)
            kt1 = min(32, 4 * c + 5)
            nkt = kt1 - kt0
            for dt in range(NDT):
                k.dma(qb[dt][0:128, 0:CH], qD[dt * 128:dt * 128 + 128, cs])
                k.dma(kw[dt][0:128, 0:nkt * 128],
                      kD[dt * 128:dt * 128 + 128, kt0 * 128:kt1 * 128])
            for j in range(nkt):
                k.dma(vw[j][0:128, 0:D], vD[(kt0 + j) * 128:(kt0 + j) * 128 + 128, 0:D])
            for h in range(16):
                hp = (h % 2) * 64
                hd = h // 2
                for j in range(nkt):
                    mi = (kt0 + j) - (4 * c - 1)
                    k.mm(psS2[0:128, 0:CH],
                         kw[hd][hp:hp + 64, j * 128:j * 128 + 128],
                         qb[hd][hp:hp + 64, 0:CH], start=True, stop=True)
                    k.tt(tf32[0:128, 0:CH], psS2[0:128, 0:CH], mneg_b[mi][0:128, 0:CH],
                         ALU.add)
                    k.act(fbf[0:128, 0:CH], tf32[0:128, 0:CH], AF.Exp)
                    k.tt(fbf[0:128, 0:CH], fbf[0:128, 0:CH], mbin_b[mi][0:128, 0:CH],
                         ALU.subtract)
                    k.mm(psO2[0:64, 0:CH], vw[j][0:128, h * 64:h * 64 + 64],
                         fbf[0:128, 0:CH], start=(j == 0), stop=False)
                    k.mm(psD2[0:1, 0:CH], onesb_col[0:128, 0:1], fbf[0:128, 0:CH],
                         start=(j == 0), stop=(j == nkt - 1))
                k.mm(psO2[0:64, 0:CH],
                     vt_rb[0:1, h * 64:h * 64 + 64], onesb_row[0:1, 0:CH],
                     start=False, stop=True)
                k.tt(dd[0:1, 0:CH], psD2[0:1, 0:CH], wcm[0:1, cs], ALU.add)
                k.op("vector", "reciprocal", rd[0:1, 0:CH], dd[0:1, 0:CH])
                k.cp(rdb[0:1, 0:CH], rd[0:1, 0:CH])
                k.mm(psRB[0:64, 0:CH], onesb_row[0:1, 0:64],
                     rdb[0:1, 0:CH], start=True, stop=True)
                k.cp(obf[0:64, 0:CH], psO2[0:64, 0:CH])
                k.tt(oT[hd][hp:hp + 64, 0:CH], obf[0:64, 0:CH], psRB[0:64, 0:CH],
                     ALU.mult)
            for dt in range(NDT):
                k.dma(xc[0][0:128, 0:CH], xT[dt * 128:dt * 128 + 128, cs])
                k.tt(sq[0:128, 0:CH], xc[0][0:128, 0:CH], oT[dt][0:128, 0:CH], ALU.add)
                k.dma(yT[dt * 128:dt * 128 + 128, cs], sq[0:128, 0:CH])
        k.build()
    return k


def build_launch_W():
    k = K()
    FF = 4096
    w1e = k.dram("w1e", [D, FF], F32, "ExternalInput")
    w2re = k.dram("w2re", [8 * FF, 128], F32, "ExternalInput")
    w1b = k.dram("w1b", [D, FF], BF16, "ExternalOutput")
    w2b = k.dram("w2b", [8 * 128, FF], BF16, "ExternalOutput")
    with k.ctx:
        sc = k.sb("sc", 128, FF, F32)
        cb = k.sb("cb", 128, FF, BF16)
        for dt in range(8):
            k.dma(sc[0:128, 0:FF], w1e[dt * 128:dt * 128 + 128, 0:FF])
            k.cp(cb[0:128, 0:FF], sc[0:128, 0:FF])
            k.dma(w1b[dt * 128:dt * 128 + 128, 0:FF], cb[0:128, 0:FF])
        for dt in range(8):
            for ft in range(32):
                k.dma(sc[0:128, ft * 128:ft * 128 + 128],
                      w2re[dt * FF + ft * 128:dt * FF + ft * 128 + 128, 0:128])
            k.cp(cb[0:128, 0:FF], sc[0:128, 0:FF])
            k.dma(w2b[dt * 128:(dt + 1) * 128, 0:FF], cb[0:128, 0:FF])
        k.build()
    return k


def build_launch_G(weights):
    global ones_f32_row_g
    k = K()
    FF = 4096
    NFT = 32
    xT = k.dram("xT", [D, S], F32, "ExternalInput")
    yT = k.dram("yT", [D, S], F32, "ExternalOutput")
    xnD = k.dram("xnD", [D, S], BF16, "Internal")
    wD = k.dram("wD", [8, S], BF16, "Internal")
    yeD = k.dram("yeD", [8 * D, S], BF16, "Internal")
    w1ball = k.dram("w1ball", [8 * D, FF], BF16, "ExternalInput")
    w2ball = k.dram("w2ball", [8 * D, FF], BF16, "ExternalInput")
    b1T = k.din("b1T", weights["b1T"])
    b2r = k.din("b2r", weights["b2"])
    e01I = k.din("e01I", np.array([[1.0]] * 2 + [[0.0]] * 126, np.float32))
    onc = k.din("onc", c_ones_f32_col())
    onr = k.din("onr", np.ones((1, 128), np.float32))
    onrb = k.din("onrb", np.ones((1, 512), bf))

    with k.ctx:
        ones_col = k.sb("ones_col", 128, 1, F32)
        ones_row = k.sb("ones_row", 1, 128, F32)
        ones_f32_row_g = ones_row
        onesb_row = k.sb("onesb_row", 1, 512, BF16)
        e01 = k.sb("e01", 128, 1, F32)
        k.dma(ones_col[0:128, 0:1], onc[0:128, 0:1])
        k.dma(ones_row[0:1, 0:128], onr[0:1, 0:128])
        k.dma(onesb_row[0:1, 0:512], onrb[0:1, 0:512])
        k.dma(e01[0:128, 0:1], e01I[0:128, 0:1])

        CP = 256
        b1s = [k.sb(f"b1s{i}", 128, 8, F32) for i in range(NFT)]
        for i in range(NFT):
            k.dma(b1s[i][0:128, 0:8], b1T[i * 128:i * 128 + 128, 0:8])
        b2one = k.sb("b2one", 1, D, BF16)
        b2f = k.sb("b2f", 1, D, F32)

        xc = [k.sb(f"xc{i}", 128, CP, F32) for i in range(NDT)]
        xn = [k.sb(f"xn{i}", 128, CH, BF16) for i in range(NDT)]
        hc = [k.sb(f"hc{i}", 128, CH, BF16) for i in range(NFT)]
        w1s = [k.sb(f"w1s{i}", 128, FF, BF16) for i in range(8)]
        w2s = [k.sb(f"w2s{i}", 128, FF, BF16) for i in range(8)]
        sq = k.sb("sq", 128, CP, F32)
        mrow = k.sb("mrow", 1, CP, F32)
        rrow = k.sb("rrow", 1, CP, F32)
        sA = k.sb("sA", 1, CP, F32)
        sB = k.sb("sB", 1, CP, F32)
        trow = k.sb("trow", 1, CP, F32)
        t1r = k.sb("t1r", 1, CP, F32)
        t2r = k.sb("t2r", 1, CP, F32)
        sgr = k.sb("sgr", 1, CP, F32)
        irow = k.sb("irow", 1, CP, I32)
        wrow = k.sb("wrow", 1, CH, BF16)
        wbb = k.sb("wbb", 128, CH, BF16)
        yc = k.sb("yc", 128, CH, BF16)
        ycf = k.sb("ycf", 128, CH, F32)

        psA = k.ps("psA", 1, CH)
        psB = k.ps("psB", 1, CH)
        psM = k.ps("psM", 128, CH)
        psR = k.ps("psR", 128, CH)
        psH = k.ps("psH", 128, CH)
        psY = k.ps("psY", 128, CH)

        for c in range(S // CP):
            cs = slice(c * CP, (c + 1) * CP)
            for dt in range(NDT):
                k.dma(xc[dt][0:128, 0:CP], xT[dt * 128:dt * 128 + 128, cs])
            emit_ln(k, xc, xn, sq, psA, psB, mrow, rrow, sA, sB, ones_col,
                    psM, psR, w=CP)
            for dt in range(NDT):
                k.dma(xnD[dt * 128:dt * 128 + 128, cs], xn[dt][0:128, 0:CP])
            k.mm(psA[0:1, 0:CP], e01[0:128, 0:1], xc[0][0:128, 0:CP])
            k.stt(trow[0:1, 0:CP], mrow[0:1, 0:CP], -2.0, psA[0:1, 0:CP],
                  ALU.mult, ALU.add)
            k.tt(trow[0:1, 0:CP], trow[0:1, 0:CP], rrow[0:1, 0:CP], ALU.mult)
            k.cp(irow[0:1, 0:CP], trow[0:1, 0:CP])
            k.cp(t1r[0:1, 0:CP], irow[0:1, 0:CP])
            k.tt(t2r[0:1, 0:CP], t1r[0:1, 0:CP], trow[0:1, 0:CP], ALU.is_gt)
            k.tsc(sgr[0:1, 0:CP], trow[0:1, 0:CP], 0.0, None, ALU.is_ge)
            k.tt(t2r[0:1, 0:CP], t2r[0:1, 0:CP], sgr[0:1, 0:CP], ALU.mult)
            k.tt(t1r[0:1, 0:CP], t1r[0:1, 0:CP], t2r[0:1, 0:CP], ALU.subtract)
            k.tt(t2r[0:1, 0:CP], t1r[0:1, 0:CP], trow[0:1, 0:CP], ALU.is_lt)
            k.tsc(sgr[0:1, 0:CP], sgr[0:1, 0:CP], -1.0, 1.0, ALU.mult, ALU.add)
            k.tt(t2r[0:1, 0:CP], t2r[0:1, 0:CP], sgr[0:1, 0:CP], ALU.mult)
            k.tt(t1r[0:1, 0:CP], t1r[0:1, 0:CP], t2r[0:1, 0:CP], ALU.add)
            k.cp(irow[0:1, 0:CP], t1r[0:1, 0:CP])
            k.tsc(irow[0:1, 0:CP], irow[0:1, 0:CP], 7, None, ALU.bitwise_and)
            k.cp(trow[0:1, 0:CP], irow[0:1, 0:CP])
            for e in range(8):
                k.tsc(t1r[0:1, 0:CP], trow[0:1, 0:CP], float(e), None, ALU.is_equal)
                k.tsc(t2r[0:1, 0:CP], trow[0:1, 0:CP], float((e - 1) % 8), None,
                      ALU.is_equal)
                k.tt(t1r[0:1, 0:CP], t1r[0:1, 0:CP], t2r[0:1, 0:CP], ALU.add)
                k.tsc(wrow[0:1, 0:CP], t1r[0:1, 0:CP], 0.5, None, ALU.mult)
                k.dma(wD[e:e + 1, cs], wrow[0:1, 0:CP])

        for e in range(8):
            for dt in range(8):
                k.dma(w1s[dt][0:128, 0:FF], w1ball[e * D + dt * 128:e * D + dt * 128 + 128, 0:FF])
                k.dma(w2s[dt][0:128, 0:FF], w2ball[e * D + dt * 128:e * D + dt * 128 + 128, 0:FF])
            k.dma(b2f[0:1, 0:D], b2r[e:e + 1, 0:D])
            k.cp(b2one[0:1, 0:D], b2f[0:1, 0:D])
            for c in range(NCH):
                cs = slice(c * CH, (c + 1) * CH)
                for dt in range(NDT):
                    k.dma(xn[dt][0:128, 0:CH], xnD[dt * 128:dt * 128 + 128, cs])
                k.dma(wrow[0:1, 0:CH], wD[e:e + 1, cs])
                k.mm(psM[0:128, 0:CH], onesb_row[0:1, 0:128], wrow[0:1, 0:CH])
                k.cp(wbb[0:128, 0:CH], psM[0:128, 0:CH])
                for ft in range(NFT):
                    for dt in range(NDT):
                        k.mm(psH[0:128, 0:CH], w1s[dt][0:128, ft * 128:ft * 128 + 128],
                             xn[dt][0:128, 0:CH], start=(dt == 0), stop=(dt == NDT - 1))
                    k.act(hc[ft][0:128, 0:CH], psH[0:128, 0:CH], AF.Relu,
                          bias=b1s[ft][0:128, e:e + 1])
                for dt in range(NDT):
                    for ft in range(NFT):
                        k.mm(psY[0:128, 0:CH],
                             w2s[dt][0:128, ft * 128:ft * 128 + 128],
                             hc[ft][0:128, 0:CH], start=(ft == 0), stop=False)
                    k.mm(psY[0:128, 0:CH],
                         b2one[0:1, dt * 128:dt * 128 + 128], onesb_row[0:1, 0:CH],
                         start=False, stop=True)
                    k.tt(yc[0:128, 0:CH], psY[0:128, 0:CH], wbb[0:128, 0:CH],
                         ALU.mult)
                    k.dma(yeD[e * D + dt * 128:e * D + dt * 128 + 128, cs],
                          yc[0:128, 0:CH])

        for c in range(NCH):
            cs = slice(c * CH, (c + 1) * CH)
            for dt in range(NDT):
                k.dma(ycf[0:128, 0:CH], xT[dt * 128:dt * 128 + 128, cs])
                for e in range(8):
                    k.dma(yc[0:128, 0:CH], yeD[e * D + dt * 128:e * D + dt * 128 + 128, cs])
                    k.tt(ycf[0:128, 0:CH], ycf[0:128, 0:CH], yc[0:128, 0:CH], ALU.add)
                k.dma(yT[dt * 128:dt * 128 + 128, cs], ycf[0:128, 0:CH])
        k.build()
    return k


def c_rope():
    HD = 64
    inv = (10000.0 ** (-np.arange(0, HD, 2, dtype=np.float64) / HD))
    ang = np.arange(S, dtype=np.float64)[:, None] * inv[None, :]
    cosv = np.cos(ang)
    sinv = np.sin(ang)
    C = np.zeros((128, S), np.float32)
    Sg = np.zeros((128, S), np.float32)
    for p in range(128):
        idx = p % 32
        sign = 1.0 if (p % 64) < 32 else -1.0
        C[p] = cosv[:, idx]
        Sg[p] = sign * sinv[:, idx]
    return C.astype(bf), Sg.astype(bf)


def c_perm():
    P = np.zeros((128, 128), np.float32)
    for i in range(128):
        sw = i + 32 if (i % 64) < 32 else i - 32
        P[sw, i] = 1.0
    return P.astype(bf)


def c_masks_H():
    m = np.zeros((4, 128, CH), np.float32)
    for j in range(4):
        kr = np.arange(128)[:, None]
        qr = np.arange(CH)[None, :]
        m[j] = np.where(128 * j + kr <= qr, 0.0, -30000.0)
    return m


def c_kvind(t):
    m = np.zeros((128, 4), np.float32)
    for p in range(128):
        g = 2 * t + (p // 64)
        m[p, g] = 1.0
    return m.astype(bf)


def build_launch_H(weights):
    global ones_f32_row_g
    k = K()
    KV = 256
    xT = k.dram("xT", [D, S], F32, "ExternalInput")
    yT = k.dram("yT", [D, S], F32, "ExternalOutput")
    qD = k.dram("qD", [D, S], BF16, "Internal")
    wq = k.din("wq", weights["WqT"])
    wk = k.din("wk", weights["WkT"])
    wv = k.din("wv", weights["WvT"])
    wp = k.din("wp", weights["WprojT"])
    qgc = k.din("qgc", weights["qg"])
    qgr = k.din("qgr", weights["qg_row"])
    CroI, SroI = c_rope()
    Cro = k.din("Cro", CroI)
    Sro = k.din("Sro", SroI)
    permI = k.din("permI", c_perm())
    mnegI = k.din("mnegI", c_masks_H().reshape(4 * 128, CH))
    indh = k.din("indh", np.concatenate([c_ind_h2d(dt) for dt in range(8)], axis=0))
    indt = k.din("indt", np.concatenate([c_ind_d2h(dt) for dt in range(8)], axis=1))
    kvh2dI = k.din("kvh2dI", np.concatenate(
        [np.ascontiguousarray(c_kvind(t).T) for t in range(2)], axis=0))
    kvd2hI = k.din("kvd2hI", np.concatenate([c_kvind(t) for t in range(2)], axis=1))
    onc = k.din("onc", c_ones_f32_col())
    onr = k.din("onr", np.ones((1, 128), np.float32))
    oncb = k.din("oncb", np.ones((128, 1), bf))
    onrb = k.din("onrb", np.ones((1, 512), bf))
    EPS = 1.1920929e-07

    with k.ctx:
        ones_col = k.sb("ones_col", 128, 1, F32)
        ones_row = k.sb("ones_row", 1, 128, F32)
        ones_f32_row_g = ones_row
        onesb_col = k.sb("onesb_col", 128, 1, BF16)
        onesb_row = k.sb("onesb_row", 1, 512, BF16)
        k.dma(ones_col[0:128, 0:1], onc[0:128, 0:1])
        k.dma(ones_row[0:1, 0:128], onr[0:1, 0:128])
        k.dma(onesb_col[0:128, 0:1], oncb[0:128, 0:1])
        k.dma(onesb_row[0:1, 0:512], onrb[0:1, 0:512])
        Ct = k.sb("Ct", 128, S, BF16)
        St = k.sb("St", 128, S, BF16)
        k.dma(Ct[0:128, 0:S], Cro[0:128, 0:S])
        k.dma(St[0:128, 0:S], Sro[0:128, 0:S])
        perm_b = k.sb("perm_b", 128, 128, BF16)
        k.dma(perm_b[0:128, 0:128], permI[0:128, 0:128])
        mneg_b = [k.sb(f"mneg{j}", 128, CH, F32) for j in range(4)]
        for j in range(4):
            k.dma(mneg_b[j][0:128, 0:CH], mnegI[j * 128:j * 128 + 128, 0:CH])
        ind_b = [k.sb(f"ind{i}", 16, 128, BF16) for i in range(8)]
        indt_b = [k.sb(f"indt{i}", 128, 16, BF16) for i in range(8)]
        for i in range(8):
            k.dma(ind_b[i][0:16, 0:128], indh[i * 16:i * 16 + 16, 0:128])
            k.dma(indt_b[i][0:128, 0:16], indt[0:128, i * 16:i * 16 + 16])
        kvh2d = [k.sb(f"kvh2d{i}", 4, 128, BF16) for i in range(2)]
        kvd2h = [k.sb(f"kvd2h{i}", 128, 4, BF16) for i in range(2)]
        for i in range(2):
            k.dma(kvh2d[i][0:4, 0:128], kvh2dI[i * 4:i * 4 + 4, 0:128])
            k.dma(kvd2h[i][0:128, 0:4], kvd2hI[0:128, i * 4:i * 4 + 4])

        wsc = k.sb("wsc", 128, D, F32)
        wq_b = [k.sb(f"wqb{i}", 128, D, BF16) for i in range(8)]
        wk_b = [k.sb(f"wkb{i}", 128, KV, BF16) for i in range(8)]
        wv_b = [k.sb(f"wvb{i}", 128, KV, BF16) for i in range(8)]
        wp_b = [k.sb(f"wpb{i}", 128, D, BF16) for i in range(8)]
        emit_wconv(k, wq, wq_b, wsc)
        emit_wconv(k, wk, wk_b, wsc)
        emit_wconv(k, wv, wv_b, wsc)
        emit_wconv(k, wp, wp_b, wsc)

        qg8 = k.sb("qg8", 16, 1, F32)
        k.dma(qg8[0:16, 0:1], qgc[0:16, 0:1])
        k.tsc(qg8[0:16, 0:1], qg8[0:16, 0:1], 0.125, None, ALU.mult)
        qga = k.sb("qga", 1, 16, F32)
        k.dma(qga[0:1, 0:16], qgr[0:1, 0:16])
        k.act(qga[0:1, 0:16], qga[0:1, 0:16], AF.Square)
        k.act(qga[0:1, 0:16], qga[0:1, 0:16], AF.Sqrt)
        k.tsc(qga[0:1, 0:16], qga[0:1, 0:16], -8.0, None, ALU.mult)
        biasc = k.sb("biasc", 128, 16, F32)

        xc = [k.sb(f"xc{i}", 128, CH, F32) for i in range(NDT)]
        xn = [k.sb(f"xn{i}", 128, CH, BF16) for i in range(NDT)]
        qt = [k.sb(f"qt{i}", 128, CH, BF16) for i in range(NDT)]
        Kt = [k.sb(f"Kt{i}", 128, S, BF16) for i in range(2)]
        Vt = [k.sb(f"Vt{i}", 128, KV, BF16) for i in range(32)]
        oT = [k.sb(f"oT{i}", 128, CH, BF16) for i in range(NDT)]
        qh = k.sb("qh", 128, CH, BF16)
        sq = k.sb("sq", 128, CH, F32)
        sq2 = k.sb("sq2", 128, CH, BF16)
        mrow = k.sb("mrow", 1, CH, F32)
        rrow = k.sb("rrow", 1, CH, F32)
        sA = k.sb("sA", 1, CH, F32)
        sB = k.sb("sB", 1, CH, F32)
        ssr = k.sb("ssr", 16, CH, F32)
        ssb = k.sb("ssb", 16, CH, BF16)
        t1b = k.sb("t1b", 128, CH, BF16)
        t2b = k.sb("t2b", 128, CH, BF16)
        pbf = k.sb("pbf", 128, CH, BF16)
        tf32 = k.sb("tf32", 128, CH, F32)
        dd = k.sb("dd", 1, CH, F32)
        rd = k.sb("rd", 1, CH, F32)
        rdb = k.sb("rdb", 1, CH, BF16)
        obf = k.sb("obf", 64, CH, F32)
        yc = k.sb("yc", 128, CH, F32)

        psA = k.ps("psA", 1, CH)
        psB = k.ps("psB", 1, CH)
        psM = k.ps("psM", 128, CH)
        psR = k.ps("psR", 128, CH)
        psO = k.ps("psO", 128, CH)
        psSS = k.ps("psSS", 16, CH)
        psPM = k.ps("psPM", 128, CH)

        k.mm(psM[0:128, 0:16], ones_row[0:1, 0:128], qga[0:1, 0:16])
        k.cp(biasc[0:128, 0:16], psM[0:128, 0:16])

        def rms_rope(tiles, nt, inds_d2h, inds_h2d, nh, gain_col, outD, c):
            cs = slice(c * CH, (c + 1) * CH)
            for dt in range(nt):
                k.tt(sq2[0:128, 0:CH], tiles[dt][0:128, 0:CH],
                     tiles[dt][0:128, 0:CH], ALU.mult)
                k.mm(psSS[0:nh, 0:CH], inds_d2h[dt][0:128, 0:nh],
                     sq2[0:128, 0:CH], start=(dt == 0), stop=(dt == nt - 1))
            k.tsc(ssr[0:nh, 0:CH], psSS[0:nh, 0:CH], 1.0 / 64, EPS, ALU.mult, ALU.add)
            k.act(ssr[0:nh, 0:CH], ssr[0:nh, 0:CH], AF.Sqrt)
            k.op("vector", "reciprocal", ssr[0:nh, 0:CH], ssr[0:nh, 0:CH])
            if gain_col is not None:
                k.tsc(ssr[0:nh, 0:CH], ssr[0:nh, 0:CH], gain_col, None, ALU.mult)
            k.cp(ssb[0:nh, 0:CH], ssr[0:nh, 0:CH])
            for dt in range(nt):
                k.mm(psM[0:128, 0:CH], inds_h2d[dt][0:nh, 0:128], ssb[0:nh, 0:CH])
                k.mm(psPM[0:128, 0:CH], perm_b[0:128, 0:128], tiles[dt][0:128, 0:CH])
                k.tt(t1b[0:128, 0:CH], tiles[dt][0:128, 0:CH], Ct[0:128, cs], ALU.mult)
                k.tt(t2b[0:128, 0:CH], psPM[0:128, 0:CH], St[0:128, cs], ALU.mult)
                k.tt(t1b[0:128, 0:CH], t1b[0:128, 0:CH], t2b[0:128, 0:CH], ALU.add)
                k.tt(outD[dt], t1b[0:128, 0:CH], psM[0:128, 0:CH], ALU.mult)

        for c in range(NCH):
            cs = slice(c * CH, (c + 1) * CH)
            for dt in range(NDT):
                k.dma(xc[dt][0:128, 0:CH], xT[dt * 128:dt * 128 + 128, cs])
            emit_ln(k, xc, xn, sq, psA, psB, mrow, rrow, sA, sB, ones_col, psM, psR)
            for ot in range(8):
                for dt in range(NDT):
                    k.mm(psO[0:128, 0:CH], wq_b[dt][0:128, ot * 128:ot * 128 + 128],
                         xn[dt][0:128, 0:CH], start=(dt == 0), stop=(dt == NDT - 1))
                k.cp(qt[ot][0:128, 0:CH], psO[0:128, 0:CH])
            rms_rope(qt, 8, indt_b, ind_b, 16, qg8[0:16, 0:1],
                     [qt[i][0:128, 0:CH] for i in range(8)], c)
            for ot in range(8):
                k.dma(qD[ot * 128:ot * 128 + 128, cs], qt[ot][0:128, 0:CH])
            for ot in range(2):
                for dt in range(NDT):
                    k.mm(psO[0:128, 0:CH], wk_b[dt][0:128, ot * 128:ot * 128 + 128],
                         xn[dt][0:128, 0:CH], start=(dt == 0), stop=(dt == NDT - 1))
                k.cp(qt[ot][0:128, 0:CH], psO[0:128, 0:CH])
            rms_rope(qt, 2, kvd2h, kvh2d, 4, None,
                     [Kt[0][0:128, cs], Kt[1][0:128, cs]], c)
            for ss in range(4):
                for dt in range(NDT):
                    k.mm(psO[0:128, 0:KV], xn[dt][0:128, ss * 128:ss * 128 + 128],
                         wv_b[dt][0:128, 0:KV], start=(dt == 0), stop=(dt == NDT - 1))
                k.cp(Vt[c * 4 + ss][0:128, 0:KV], psO[0:128, 0:KV])

        psS2, psO2, psD2, psRB = psM, psO, psA, psR
        for c in range(NCH):
            cs = slice(c * CH, (c + 1) * CH)
            ktmax = 4 * c + 3
            for h in range(16):
                hp = (h % 2) * 64
                hd = h // 2
                g = h // 4
                gt = g // 2
                gp = (g % 2) * 64
                k.dma(qh[gp:gp + 64, 0:CH], qD[h * 64:h * 64 + 64, cs])
                for kt in range(ktmax + 1):
                    k.mm(psS2[0:128, 0:CH],
                         Kt[gt][gp:gp + 64, kt * 128:kt * 128 + 128],
                         qh[gp:gp + 64, 0:CH], start=True, stop=True)
                    if kt >= 4 * c:
                        j = kt - 4 * c
                        k.tt(tf32[0:128, 0:CH], psS2[0:128, 0:CH],
                             mneg_b[j][0:128, 0:CH], ALU.add)
                        k.act(pbf[0:128, 0:CH], tf32[0:128, 0:CH], AF.Exp,
                              bias=biasc[0:128, h:h + 1])
                    else:
                        k.act(pbf[0:128, 0:CH], psS2[0:128, 0:CH], AF.Exp,
                              bias=biasc[0:128, h:h + 1])
                    k.mm(psO2[0:64, 0:CH], Vt[kt][0:128, g * 64:g * 64 + 64],
                         pbf[0:128, 0:CH], start=(kt == 0), stop=(kt == ktmax))
                    k.mm(psD2[0:1, 0:CH], onesb_col[0:128, 0:1],
                         pbf[0:128, 0:CH], start=(kt == 0), stop=(kt == ktmax))
                k.cp(dd[0:1, 0:CH], psD2[0:1, 0:CH])
                k.op("vector", "reciprocal", rd[0:1, 0:CH], dd[0:1, 0:CH])
                k.cp(rdb[0:1, 0:CH], rd[0:1, 0:CH])
                k.mm(psRB[0:64, 0:CH], onesb_row[0:1, 0:64], rdb[0:1, 0:CH],
                     start=True, stop=True)
                k.cp(obf[0:64, 0:CH], psO2[0:64, 0:CH])
                k.tt(oT[hd][hp:hp + 64, 0:CH], obf[0:64, 0:CH],
                     psRB[0:64, 0:CH], ALU.mult)
            for dt in range(NDT):
                for kt in range(8):
                    k.mm(psPM[0:128, 0:CH], wp_b[kt][0:128, dt * 128:dt * 128 + 128],
                         oT[kt][0:128, 0:CH], start=(kt == 0), stop=(kt == 7))
                k.dma(xc[0][0:128, 0:CH], xT[dt * 128:dt * 128 + 128, cs])
                k.tt(yc[0:128, 0:CH], psPM[0:128, 0:CH], xc[0][0:128, 0:CH], ALU.add)
                k.dma(yT[dt * 128:dt * 128 + 128, cs], yc[0:128, 0:CH])
        k.build()
    return k


def _run(k, per_core_inputs, core_ids=None):
    in_maps = []
    for c in range(NCORES):
        m = dict(k.inputs)
        m.update(per_core_inputs[c])
        in_maps.append(m)
    res = run_bass_kernel_spmd(k.nc, in_maps, core_ids=list(range(NCORES)))
    return res.results


def run_CD(inp, xTs):
    wC = {
        "WqT": np.ascontiguousarray(inp["L0_attn_b_Wq_w"].T),
        "WkT": np.ascontiguousarray(inp["L0_attn_b_Wk_w"].T),
        "WvT": np.ascontiguousarray(inp["L0_attn_b_Wv_w"].T),
        "WwT": np.ascontiguousarray(inp["L0_attn_b_Ww_w"].T),
        "WrT": np.ascontiguousarray(inp["L0_attn_b_Wr_w"].T),
        "bq": inp["L0_attn_b_Wq_b"].reshape(-1, 1),
        "bk": inp["L0_attn_b_Wk_b"].reshape(-1, 1),
        "bv": inp["L0_attn_b_Wv_b"].reshape(-1, 1),
        "bw": inp["L0_attn_b_Ww_b"].reshape(-1, 1),
        "br": inp["L0_attn_b_Wr_b"].reshape(-1, 1),
    }
    kC = build_launch_C(wC)
    res = _run(kC, [{"xT": xTs[b]} for b in range(8)])
    qs = [res[b]["qT"] for b in range(8)]
    ks = [res[b]["kT"] for b in range(8)]
    us = [res[b]["uT"] for b in range(8)]
    rs = [res[b]["rT"] for b in range(8)]

    wDd = {
        "WoT": np.ascontiguousarray(inp["L0_attn_b_Wo_w"].T),
        "bo": inp["L0_attn_b_Wo_b"].reshape(-1, 1),
    }
    kD = build_launch_D(wDd)
    percore = []
    for c in range(8):
        cs = slice(c * CH, (c + 1) * CH)
        percore.append({
            "qA": np.concatenate([qs[b][:, cs] for b in range(8)], axis=0),
            "kA": np.concatenate([ks[b][:, cs] for b in range(8)], axis=0),
            "uA": np.concatenate([us[b][:, cs] for b in range(8)], axis=0),
            "rA": np.concatenate([rs[b][:, cs] for b in range(8)], axis=0),
            "xA": np.ascontiguousarray(
                np.concatenate([xTs[b][:, cs] for b in range(8)], axis=0)),
        })
    res = _run(kD, percore)
    out = []
    for b in range(8):
        out.append(np.concatenate(
            [res[c]["yA"][b * D:(b + 1) * D, :] for c in range(8)], axis=1))
    return out


def run_G(inp, xTs, w1b_cores, w2b_cores):
    wG = {
        "b1T": np.ascontiguousarray(inp["L1_ffn_b1"].T),
        "b2": inp["L1_ffn_b2"],
    }
    kG = build_launch_G(wG)
    w1ball = np.concatenate(w1b_cores, axis=0)
    w2ball = np.concatenate(w2b_cores, axis=0)
    res = _run(kG, [{"xT": xTs[b], "w1ball": w1ball, "w2ball": w2ball}
                    for b in range(8)])
    return [res[b]["yT"] for b in range(8)]


def kernel(**inp):
    x = np.ascontiguousarray(inp["L0_attn_a_x"])
    xTs = [np.ascontiguousarray(x[b].T) for b in range(8)]

    wA = {
        "W_in": inp["L0_attn_a_W_in"],
        "b_in": inp["L0_attn_a_b_in"].reshape(-1, 1),
        "W_g": inp["L0_attn_a_W_g"],
        "b_g": inp["L0_attn_a_b_g"].reshape(-1, 1),
        "W_out": inp["L0_attn_a_W_out"],
        "b_out": inp["L0_attn_a_b_out"].reshape(-1, 1),
    }
    kA = build_launch_A(wA)
    res = _run(kA, [{"xT": xTs[b]} for b in range(8)])
    xTs = [res[b]["yT"] for b in range(8)]

    wB = {
        "wgT": np.ascontiguousarray(inp["L0_ffn_w_gate"].T),
        "wuT": np.ascontiguousarray(inp["L0_ffn_w_up"].T),
        "wdT": np.ascontiguousarray(inp["L0_ffn_w_down"].T),
    }
    kB = build_launch_B(wB)
    res = _run(kB, [{"xT": xTs[b]} for b in range(8)])
    xTs = [res[b]["yT"] for b in range(8)]

    xTs = run_CD(inp, xTs)

    res = _run(kB, [{"xT": xTs[b]} for b in range(8)])
    xTs = [res[b]["yT"] for b in range(8)]

    wF = {
        "WqT": np.ascontiguousarray(inp["L1_attn_a_Wq"].T),
        "WkT": np.ascontiguousarray(inp["L1_attn_a_Wk"].T),
        "WvT": np.ascontiguousarray(inp["L1_attn_a_Wv"].T),
        "bq": inp["L1_attn_a_bq"].reshape(-1, 1),
        "bk": inp["L1_attn_a_bk"].reshape(-1, 1),
        "bv_row": inp["L1_attn_a_bv"].reshape(1, -1),
    }
    kF = build_launch_F(wF)
    res = _run(kF, [{"xT": xTs[b]} for b in range(8)])
    xTs = [res[b]["yT"] for b in range(8)]

    kW = build_launch_W()
    W1 = inp["L1_ffn_W1"]
    W2 = inp["L1_ffn_W2"]
    percore = []
    for e in range(8):
        w2re = np.ascontiguousarray(
            W2[e].reshape(4096, 8, 128).transpose(1, 0, 2).reshape(8 * 4096, 128))
        percore.append({"w1e": np.ascontiguousarray(W1[e]), "w2re": w2re})
    resW = _run(kW, percore)
    w1b_cores = [resW[e]["w1b"] for e in range(8)]
    w2b_cores = [resW[e]["w2b"] for e in range(8)]

    xTs = run_G(inp, xTs, w1b_cores, w2b_cores)

    wH = {
        "WqT": np.ascontiguousarray(inp["L1_attn_b_Wq"].T),
        "WkT": np.ascontiguousarray(inp["L1_attn_b_Wk"].T),
        "WvT": np.ascontiguousarray(inp["L1_attn_b_Wv"].T),
        "WprojT": np.ascontiguousarray(inp["L1_attn_b_Wproj"].T),
        "qg": inp["L1_attn_b_q_gain"].reshape(-1, 1),
        "qg_row": inp["L1_attn_b_q_gain"].reshape(1, -1),
    }
    kH = build_launch_H(wH)
    res = _run(kH, [{"xT": xTs[b]} for b in range(8)])
    xTs = [res[b]["yT"] for b in range(8)]

    xTs = run_G(inp, xTs, w1b_cores, w2b_cores)

    out = np.stack([np.ascontiguousarray(np.asarray(t).T) for t in xTs], axis=0)
    return out.astype(np.float32)
